# version 01
# baseline (speedup 1.0000x reference)
import numpy as np
from contextlib import ExitStack

import concourse.bass as bass
import concourse.mybir as mybir
from concourse.bass_utils import run_bass_kernel_spmd
from concourse.alu_op_type import AluOpType as OP

F32 = mybir.dt.float32
BF16 = mybir.dt.bfloat16
I32 = mybir.dt.int32
AF = mybir.ActivationFunctionType
AX = mybir.AxisListType.X

NCORES = 8
TOK = 2048
TC = 16
D = 4096
KC = 32
NTN = 16
NTW = 256
TG = 4
EPS = 1e-5
CAPACITY = float(int(3.0 * (NCORES * TOK) / 8))
BIG_NEG = -1.0e30
ALPHA = 1.0 / 16.0


class Sem:

    def __init__(self, es, nc, name):
        self.h = es.enter_context(nc.semaphore(name))
        self.v = 0

    def inc(self, instr, k=1):
        instr.then_inc(self.h, k)
        self.v += k
        return instr


def ln_chain(V, A, x_ap, out_ap, st, mv, nm, nmr, rstd, eps_t, sv, sa, nstats):
    w = D // nstats
    for i in range(nstats):
        V.bn_stats(st[:, 6 * i:6 * i + 6], x_ap[:, w * i:w * (i + 1)])
    V.drain()
    V.bn_aggr(mv[:, :], st[:, 0:6 * nstats])
    V.drain()
    sv.inc(V.tensor_scalar(nm[:, :], mv[:, 0:1], -1.0, None, OP.mult))
    A.wait_ge(sv.h, sv.v)
    A.activation(rstd[:, :], mv[:, 1:2], AF.Ln, bias=eps_t[:, 0:1], scale=1.0)
    A.drain()
    sa.inc(A.activation(rstd[:, :], rstd[:, :], AF.Exp, scale=-0.5))
    V.wait_ge(sa.h, sa.v)
    V.tensor_tensor(nmr[:, :], nm[:, :], rstd[:, :], OP.mult)
    V.drain()
    return V.tensor_scalar(out_ap, x_ap, rstd[:, 0:1], nmr[:, 0:1], OP.mult, OP.add)



def build_A():
    nc = bass.Bass("TRN2", target_bir_lowering=False, debug=False)

    x_d = nc.dram_tensor("x", [TOK, D], F32, kind="ExternalInput").ap()
    waT_d = nc.dram_tensor("waT", [D, D], F32, kind="ExternalInput").ap()
    maskR_d = nc.dram_tensor("maskR", [128, 128, 1024], I32, kind="ExternalInput").ap()
    aRe_d = nc.dram_tensor("aRe", [D, 128], F32, kind="ExternalInput").ap()
    gw1T_d = nc.dram_tensor("gw1T", [D, 32], F32, kind="ExternalInput").ap()
    gw2T_d = nc.dram_tensor("gw2T", [32, 8], F32, kind="ExternalInput").ap()
    gb1B_d = nc.dram_tensor("gb1B", [128, 32], F32, kind="ExternalInput").ap()
    gamB_d = nc.dram_tensor("gamB", [128, 32], F32, kind="ExternalInput").ap()
    betB_d = nc.dram_tensor("betB", [128, 32], F32, kind="ExternalInput").ap()
    gb2B_d = nc.dram_tensor("gb2B", [128, 8], F32, kind="ExternalInput").ap()

    cnt_d = nc.dram_tensor("counts", [16, 1], F32, kind="ExternalOutput").ap()
    x2_d = nc.dram_tensor("x2o", [TOK, D], F32, kind="ExternalOutput").ap()
    l2t_d = nc.dram_tensor("l2to", [TOK, D], BF16, kind="ExternalOutput").ap()
    t1_d = nc.dram_tensor("t1o", [128, TOK], BF16, kind="ExternalOutput").ap()
    t2_d = nc.dram_tensor("t2o", [128, TOK], BF16, kind="ExternalOutput").ap()

    b_dram = nc.dram_tensor("b_scr", [TOK, D], BF16)
    bin_dram = nc.dram_tensor("bin_scr", [128, 128], BF16)

    PE, V, A, G, SY, SC = nc.tensor, nc.vector, nc.scalar, nc.gpsimd, nc.sync, nc.scalar

    with ExitStack() as es:
        e = es.enter_context

        ident = e(nc.sbuf_tensor("ident", [128, 128], BF16))
        e4t = e(nc.sbuf_tensor("e4t", [4, 128], BF16))
        eps_t = e(nc.sbuf_tensor("eps_t", [128, 1], F32))
        ones_b = e(nc.sbuf_tensor("ones_b", [128, 1], BF16))
        gw1_sb = e(nc.sbuf_tensor("gw1_sb", [128, KC * 32], BF16))
        A_sb = e(nc.sbuf_tensor("A_sb", [128, KC * 128], BF16))
        gw2_sb = e(nc.sbuf_tensor("gw2_sb", [32, 8], BF16))
        gb1B = e(nc.sbuf_tensor("gb1B_s", [128, 32], F32))
        gamB = e(nc.sbuf_tensor("gamB_s", [128, 32], F32))
        betB = e(nc.sbuf_tensor("betB_s", [128, 32], F32))
        gb2B = e(nc.sbuf_tensor("gb2B_s", [128, 8], F32))
        bs_all = e(nc.sbuf_tensor("bs_all", [128, 128], F32))
        binX = e(nc.sbuf_tensor("binX", [128, 128], BF16))
        t1_sb = e(nc.sbuf_tensor("t1_sb", [128, TOK], BF16))
        t2_sb = e(nc.sbuf_tensor("t2_sb", [128, TOK], BF16))
        cnt_acc = e(nc.sbuf_tensor("cnt_acc", [16, 1], F32))
        cnt_tmp = e(nc.sbuf_tensor("cnt_tmp", [16, 1], F32))
        stats = [e(nc.sbuf_tensor(f"st{i}", [128, 96], F32)) for i in range(2)]
        mv = [e(nc.sbuf_tensor(f"mv{i}", [128, 2], F32)) for i in range(2)]
        nm = [e(nc.sbuf_tensor(f"nm{i}", [128, 1], F32)) for i in range(2)]
        nmr = [e(nc.sbuf_tensor(f"nmr{i}", [128, 1], F32)) for i in range(2)]
        rstd = [e(nc.sbuf_tensor(f"rs{i}", [128, 1], F32)) for i in range(2)]
        h_sb = e(nc.sbuf_tensor("h_sb", [128, 32], F32))
        hn_sb = e(nc.sbuf_tensor("hn_sb", [128, 32], F32))
        hb_sb = e(nc.sbuf_tensor("hb_sb", [128, 32], BF16))
        hT_sb = e(nc.sbuf_tensor("hT_sb", [32, 128], BF16))
        gates = e(nc.sbuf_tensor("gates", [128, 8], F32))
        gates2 = e(nc.sbuf_tensor("gates2", [128, 8], F32))
        m1 = e(nc.sbuf_tensor("m1", [128, 1], F32))
        m2 = e(nc.sbuf_tensor("m2", [128, 1], F32))
        eqb = e(nc.sbuf_tensor("eqb", [128, 16], BF16))
        eq1f = e(nc.sbuf_tensor("eq1f", [128, 8], F32))
        eq2f = e(nc.sbuf_tensor("eq2f", [128, 8], F32))
        dgap = e(nc.sbuf_tensor("dgap", [128, 1], F32))
        sexp = e(nc.sbuf_tensor("sexp", [128, 1], F32))
        sp1 = e(nc.sbuf_tensor("sp1", [128, 1], F32))
        rs_ = e(nc.sbuf_tensor("rs_", [128, 1], F32))
        p1 = e(nc.sbuf_tensor("p1", [128, 1], F32))
        p2 = e(nc.sbuf_tensor("p2", [128, 1], F32))
        g1 = e(nc.sbuf_tensor("g1", [128, 8], F32))
        g2 = e(nc.sbuf_tensor("g2", [128, 8], F32))
        tmp_f = e(nc.sbuf_tensor("tmp_f", [128, 128], F32))
        tmp1b = e(nc.sbuf_tensor("tmp1b", [128, 128], BF16))
        tmp2b = e(nc.sbuf_tensor("tmp2b", [128, 128], BF16))
        hg_st = e(nc.sbuf_tensor("hg_st", [128, 6], F32))
        hg_mv = e(nc.sbuf_tensor("hg_mv", [128, 2], F32))
        hg_nm = e(nc.sbuf_tensor("hg_nm", [128, 1], F32))
        hg_nmr = e(nc.sbuf_tensor("hg_nmr", [128, 1], F32))
        hg_rs = e(nc.sbuf_tensor("hg_rs", [128, 1], F32))

        s_const = Sem(es, nc, "s_const")
        s_ld1 = Sem(es, nc, "s_ld1")
        s_msk = [Sem(es, nc, f"s_msk{i}") for i in range(2)]
        s_bin = Sem(es, nc, "s_bin")
        s_binp = Sem(es, nc, "s_binp")
        s_binpm = Sem(es, nc, "s_binpm")
        s_ldx = Sem(es, nc, "s_ldx")
        s_abf = [Sem(es, nc, f"s_abf{i}") for i in range(2)]
        s_aused = Sem(es, nc, "s_aused")
        s_lnv = Sem(es, nc, "s_lnv")
        s_lna = Sem(es, nc, "s_lna")
        s_pt = Sem(es, nc, "s_pt")
        s_ev = Sem(es, nc, "s_ev")
        s_wm = [Sem(es, nc, f"s_wm{i}") for i in range(2)]
        s_wmf = [Sem(es, nc, f"s_wmf{i}") for i in range(2)]
        s_wa = [Sem(es, nc, f"s_wa{i}") for i in range(2)]
        s_mm = Sem(es, nc, "s_mm")
        s_aev = Sem(es, nc, "s_aev")
        s_stg = [Sem(es, nc, f"s_stg{i}") for i in range(8)]
        s_pb = [Sem(es, nc, f"s_pb{i}") for i in range(2)]
        s_px = [Sem(es, nc, f"s_px{i}") for i in range(2)]
        s_v4 = Sem(es, nc, "s_v4")
        s_a4 = Sem(es, nc, "s_a4")
        s_p4 = Sem(es, nc, "s_p4")
        s_pe4 = Sem(es, nc, "s_pe4")
        s_tp4 = Sem(es, nc, "s_tp4")
        s_l2u = Sem(es, nc, "s_l2u")
        s_x2s = [Sem(es, nc, f"s_x2s{i}") for i in range(2)]
        s_l2 = [Sem(es, nc, f"s_l2{i}") for i in range(2)]
        s_out = Sem(es, nc, "s_out")

        blk = e(nc.Block())

        G.memset(ident[:, :], 0.0)
        G.affine_select(out=ident[:, :], in_=ident[:, :], compare_op=OP.not_equal,
                        fill=1.0, base=0, pattern=[[-1, 128]], channel_multiplier=1)
        G.memset(e4t[:, :], 1.0)
        G.affine_select(out=e4t[:, :], in_=e4t[:, :], compare_op=OP.is_ge,
                        fill=0.0, base=0, pattern=[[1, 128]], channel_multiplier=-32)
        G.affine_select(out=e4t[:, :], in_=e4t[:, :], compare_op=OP.is_ge,
                        fill=0.0, base=31, pattern=[[-1, 128]], channel_multiplier=32)
        G.memset(ones_b[:, :], 1.0)
        s_const.inc(G.dma_start(
            out=gw1_sb[:, :].rearrange("p (kc g) -> p kc g", kc=KC),
            in_=gw1T_d.rearrange("(kc kp) g -> kp kc g", kp=128)), 16)
        s_const.inc(G.dma_start(
            out=A_sb[:, :].rearrange("p (kc er) -> p kc er", kc=KC),
            in_=aRe_d.rearrange("(kc kp) er -> kp kc er", kp=128)), 16)
        s_const.inc(G.dma_start(out=gw2_sb[:, :], in_=gw2T_d), 16)
        s_ld1.inc(SY.dma_start(out=gb1B[:, :], in_=gb1B_d), 16)
        s_ld1.inc(SY.dma_start(out=gamB[:, :], in_=gamB_d), 16)
        s_ld1.inc(SY.dma_start(out=betB[:, :], in_=betB_d), 16)
        s_ld1.inc(SY.dma_start(out=gb2B[:, :], in_=gb2B_d), 16)
        V.memset(eps_t[:, :], EPS)
        V.memset(cnt_acc[:, :], 0.0)
        V.drain()

        es_big = ExitStack()
        aT = es_big.enter_context(nc.sbuf_tensor("aT", [128, KC * TOK], BF16))
        binP = es_big.enter_context(nc.sbuf_tensor("binP", [128, 4096], BF16))
        es_bst = ExitStack()

        es_msk = ExitStack()
        msk = [es_msk.enter_context(nc.sbuf_tensor(f"msk{i}", [128, 4096], BF16))
               for i in range(2)]


        with ExitStack() as es2:
            xbuf = es2.enter_context(nc.sbuf_tensor("xb", [128, D], F32))
            abf_one = es2.enter_context(nc.sbuf_tensor("ab0", [128, D], BF16))
            abf = [abf_one, abf_one]
            ps_tp = [es2.enter_context(nc.psum_tensor(f"ps_tp{i}", [128, 1024], BF16))
                     for i in range(2)]

            for tc in range(TC):
                sl = tc % 2
                if tc >= 1:
                    SY.wait_ge(s_abf[(tc - 1) % 2].h, s_abf[(tc - 1) % 2].v)
                s_ldx.inc(SY.dma_start(
                    out=xbuf[:, :], in_=x_d[tc * 128:(tc + 1) * 128, :]), 16)
                V.wait_ge(s_ldx.h, s_ldx.v)
                if tc >= 1:
                    V.wait_ge(s_aused.h, tc)
                ap = ln_chain(V, A, xbuf[:, :], abf[sl][:, :], stats[sl], mv[sl],
                              nm[sl], nmr[sl], rstd[sl], eps_t, s_lnv, s_lna, 8)
                s_abf[sl].inc(ap)
                if tc == 0:
                    PE.wait_ge(s_const.h, s_const.v)
                PE.wait_ge(s_abf[sl].h, s_abf[sl].v)
                for kc in range(KC):
                    j = tc * KC + kc
                    if j >= 2:
                        PE.wait_ge(s_ev.h, j - 1)
                    s_pt.inc(PE.transpose(ps_tp[j % 2][:, 0:128],
                                          abf[sl][:, kc * 128:(kc + 1) * 128],
                                          ident[:, :]))
                    if kc == KC - 1:
                        s_aused.inc(PE.nop())
                    A.wait_ge(s_pt.h, s_pt.v)
                    s_ev.inc(A.copy(
                        aT[:, kc * TOK + tc * 128:kc * TOK + (tc + 1) * 128],
                        ps_tp[j % 2][:, 0:128]))
                for mc in (2 * tc, 2 * tc + 1):
                    msl = mc % 2
                    if mc >= 2:
                        G.wait_ge(s_msk[msl].h, s_msk[msl].v)
                    s_msk[msl].inc(G.dma_start(
                        out=msk[msl][:, :].rearrange("p (a j) -> p a j", a=4),
                        in_=maskR_d[:, 4 * mc:4 * (mc + 1), :]), 16)
                    V.wait_ge(s_msk[msl].h, s_msk[msl].v)
                    s_msk[msl].inc(V.tensor_reduce(
                        bs_all[:, 4 * mc:4 * (mc + 1)],
                        msk[msl][:, :].rearrange("p (a j) -> p a j", a=4),
                        axis=AX, op=OP.add))
                    V.drain()

        V.wait_ge(s_msk[0].h, s_msk[0].v)
        V.wait_ge(s_msk[1].h, s_msk[1].v)
        s_bin.inc(V.tensor_scalar(binX[:, :], bs_all[:, :], 0.0, None, OP.is_gt))
        SC.wait_ge(s_bin.h, s_bin.v)
        s_bin.inc(SC.dma_start(out=bin_dram.ap(), in_=binX[:, :]), 16)
        with ExitStack() as esb:
            bin_g = esb.enter_context(nc.sbuf_tensor("bin_g", [4, 4096], BF16))
            ps_bin = [esb.enter_context(nc.psum_tensor(f"ps_bin{i}", [128, 512], F32))
                      for i in range(2)]
            SY.wait_ge(s_bin.h, s_bin.v)
            s_bin.inc(SY.dma_start(
                out=bin_g[:, :].rearrange("g (kc nb) -> g kc nb", kc=KC),
                in_=bin_dram.ap().rearrange("(g kc) nb -> g kc nb", g=4)), 16)
            PE.wait_ge(s_bin.h, s_bin.v)
            PE.wait_ge(s_const.h, s_const.v)
            bgv = bin_g[:, :].rearrange("g (kc nb) -> g kc nb", kc=KC)
            for nt in range(NTN):
                if nt >= 2:
                    PE.wait_ge(s_binp.h, nt - 1)
                s_binpm.inc(PE.matmul(ps_bin[nt % 2][:, 0:256], e4t[:, :],
                                      bgv[:, :, nt * 8:(nt + 1) * 8],
                                      start=True, stop=True))
                A.wait_ge(s_binpm.h, s_binpm.v)
                s_binp.inc(A.copy(binP[:, nt * 256:(nt + 1) * 256],
                                  ps_bin[nt % 2][:, 0:256]))
        es_msk.close()

        bstats = es_bst.enter_context(nc.sbuf_tensor("bstats", [128, TC * NTN * 6], F32, side="right"))
        with ExitStack() as es3:
            wm = [es3.enter_context(nc.sbuf_tensor(f"wm{i}", [128, KC * NTW], BF16))
                  for i in range(2)]
            stage = [es3.enter_context(nc.sbuf_tensor(f"stg{i}", [128, NTW], BF16))
                     for i in range(8)]
            ps_acc = [es3.enter_context(nc.psum_tensor(f"ps_acc{i}", [128, 512], F32))
                      for i in range(8)]

            grp = 0
            for nt in range(NTN):
                sl = nt % 2
                if nt == 0:
                    G.wait_ge(s_binp.h, s_binp.v)
                    G.wait_ge(s_msk[0].h, s_msk[0].v)
                    G.wait_ge(s_msk[1].h, s_msk[1].v)
                if nt >= 2:
                    G.wait_ge(s_wmf[sl].h, s_wmf[sl].v)
                s_wa[sl].inc(G.dma_start(
                    out=wm[sl][:, :].rearrange("p (kc n) -> p kc n", kc=KC),
                    in_=waT_d[:, nt * NTW:(nt + 1) * NTW]
                        .rearrange("(kc kp) n -> kp kc n", kp=128)), 16)
                APL = V if nt % 2 == 0 else G
                APL.wait_ge(s_wa[sl].h, s_wa[sl].v)
                APL.wait_ge(s_binp.h, s_binp.v)
                bview = (binP[:, nt * 256:(nt + 1) * 256]
                         .rearrange("p (kc nb) -> p kc nb", kc=KC)
                         .unsqueeze(3).broadcast_to([128, KC, 8, 32]))
                wview = wm[sl][:, :].rearrange("p (kc nb n) -> p kc nb n", kc=KC, nb=8)
                s_wm[sl].inc(APL.tensor_tensor(wview, wview, bview, OP.mult))

                PE.wait_ge(s_wm[sl].h, s_wm[sl].v)
                for tg in range(TG):
                    bank = (grp % 2) * 4
                    if grp == 0:
                        PE.wait_ge(s_ev.h, s_ev.v)
                        PE.wait_ge(s_binp.h, s_binp.v)
                    if grp >= 2:
                        PE.wait_ge(s_aev.h, 4 * (grp - 1))
                    mmi = None
                    for kc in range(KC):
                        for tci in range(4):
                            tc = tg * 4 + tci
                            mmi = PE.matmul(
                                ps_acc[bank + tci][:, 0:NTW],
                                aT[:, kc * TOK + tc * 128:kc * TOK + (tc + 1) * 128],
                                wm[sl][:, kc * NTW:(kc + 1) * NTW],
                                start=(kc == 0), stop=(kc == KC - 1))
                    s_mm.inc(mmi)
                    if tg == TG - 1:
                        s_wmf[sl].inc(PE.nop())
                    A.wait_ge(s_mm.h, s_mm.v)
                    for tci in range(4):
                        tc = tg * 4 + tci
                        ssl = (grp % 2) * 4 + tci
                        uses = grp // 2
                        if uses >= 1:
                            A.wait_ge(s_stg[ssl].h, 17 * uses)
                        s_aev.inc(A.copy(stage[ssl][:, :], ps_acc[bank + tci][:, 0:NTW]))
                        SC.wait_ge(s_aev.h, s_aev.v)
                        s_stg[ssl].inc(SC.dma_start(
                            out=b_dram.ap()[tc * 128:(tc + 1) * 128,
                                            nt * NTW:(nt + 1) * NTW],
                            in_=stage[ssl][:, :]), 16)
                        V.wait_ge(s_aev.h, s_aev.v)
                        s_stg[ssl].inc(V.bn_stats(
                            bstats[:, (tc * NTN + nt) * 6:(tc * NTN + nt) * 6 + 6],
                            stage[ssl][:, :]))
                    grp += 1
        es_big.close()

        with ExitStack() as es4:
            bbuf = [es4.enter_context(nc.sbuf_tensor(f"bb{i}", [128, D], BF16))
                    for i in range(2)]
            xbuf4 = [es4.enter_context(nc.sbuf_tensor(f"xb4{i}", [128, D], F32))
                     for i in range(2)]
            c_sb = es4.enter_context(nc.sbuf_tensor("c_sb", [128, D], F32))
            x2_sb = [es4.enter_context(nc.sbuf_tensor(f"x2_{i}", [128, D], F32))
                     for i in range(2)]
            l2bf = [es4.enter_context(nc.sbuf_tensor(f"l2b{i}", [128, D], BF16))
                    for i in range(2)]
            l2t = [es4.enter_context(nc.sbuf_tensor(f"l2t{i}", [128, D], BF16))
                   for i in range(2)]
            ps_tp4 = [es4.enter_context(nc.psum_tensor(f"ps_tp4{i}", [128, 1024], BF16))
                      for i in range(2)]
            ps_h = es4.enter_context(nc.psum_tensor("ps_h", [128, 512], F32))
            ps_tmp = es4.enter_context(nc.psum_tensor("ps_tmp", [128, 512], F32))
            ps_hT = es4.enter_context(nc.psum_tensor("ps_hT", [32, 1024], BF16))
            ps_g = es4.enter_context(nc.psum_tensor("ps_g", [128, 512], F32))
            ps_cnt = es4.enter_context(nc.psum_tensor("ps_cnt", [16, 512], F32))

            SY.wait_ge(s_aev.h, s_aev.v)
            for i in range(8):
                SY.wait_ge(s_stg[i].h, s_stg[i].v)

            pt4 = 0
            v4_prev = 0
            for tc in range(TC):
                sl = tc % 2
                if tc >= 2:
                    SY.wait_ge(s_v4.h, s_v4.v)
                    SY.wait_ge(s_p4.h, s_p4.v)
                s_pb[sl].inc(SY.dma_start(
                    out=bbuf[sl][:, :],
                    in_=b_dram.ap()[tc * 128:(tc + 1) * 128, :]), 16)
                s_px[sl].inc(SY.dma_start(
                    out=xbuf4[sl][:, :], in_=x_d[tc * 128:(tc + 1) * 128, :]), 16)

                V.wait_ge(s_pb[sl].h, s_pb[sl].v)
                V.bn_aggr(mv[sl][:, :], bstats[:, tc * 96:(tc + 1) * 96])
                V.drain()
                s_v4.inc(V.tensor_scalar(nm[sl][:, :], mv[sl][:, 0:1], -1.0, None, OP.mult))
                A.wait_ge(s_v4.h, s_v4.v)
                A.activation(rstd[sl][:, :], mv[sl][:, 1:2], AF.Ln,
                             bias=eps_t[:, 0:1], scale=1.0)
                A.drain()
                s_a4.inc(A.activation(rstd[sl][:, :], rstd[sl][:, :], AF.Exp, scale=-0.5))
                V.wait_ge(s_a4.h, s_a4.v)
                V.tensor_tensor(nmr[sl][:, :], nm[sl][:, :], rstd[sl][:, :], OP.mult)
                V.drain()
                V.wait_ge(s_p4.h, s_p4.v)
                s_v4.inc(V.tensor_scalar(c_sb[:, :], bbuf[sl][:, :],
                                         rstd[sl][:, 0:1], nmr[sl][:, 0:1],
                                         OP.mult, OP.add))
                G.wait_ge(s_v4.h, s_v4.v)
                G.wait_ge(s_px[sl].h, s_px[sl].v)
                if tc >= 2:
                    G.wait_ge(s_x2s[sl].h, s_x2s[sl].v)
                s_p4.inc(G.tensor_tensor(x2_sb[sl][:, :], xbuf4[sl][:, :],
                                         c_sb[:, :], OP.add))
                SC.wait_ge(s_p4.h, s_p4.v)
                s_x2s[sl].inc(SC.dma_start(
                    out=x2_d[tc * 128:(tc + 1) * 128, :], in_=x2_sb[sl][:, :]), 16)

                V.wait_ge(s_p4.h, s_p4.v)
                if tc >= 2:
                    V.wait_ge(s_l2u.h, tc - 1)
                ap2 = ln_chain(V, A, x2_sb[sl][:, :], l2bf[sl][:, :], stats[sl],
                               mv[sl], nm[sl], nmr[sl], rstd[sl], eps_t, s_v4, s_a4, 8)
                s_v4.inc(ap2)

                PE.wait_ge(s_v4.h, s_v4.v)
                for kc in range(KC):
                    if pt4 >= 2:
                        PE.wait_ge(s_tp4.h, pt4 - 1)
                    s_pe4.inc(PE.transpose(ps_tp4[pt4 % 2][:, 0:128],
                                           l2bf[sl][:, kc * 128:(kc + 1) * 128],
                                           ident[:, :]))
                    if kc == KC - 1:
                        s_l2u.inc(PE.nop())
                    A.wait_ge(s_pe4.h, s_pe4.v)
                    if kc == 0 and tc >= 2:
                        A.wait_ge(s_l2[sl].h, s_l2[sl].v)
                    s_tp4.inc(A.copy(l2t[sl][:, kc * 128:(kc + 1) * 128],
                                     ps_tp4[pt4 % 2][:, 0:128]))
                    pt4 += 1
                SC.wait_ge(s_tp4.h, s_tp4.v)
                s_l2[sl].inc(SC.dma_start(
                    out=l2t_d[tc * 128:(tc + 1) * 128, :], in_=l2t[sl][:, :]), 16)

                PE.wait_ge(s_tp4.h, s_tp4.v)
                PE.wait_ge(s_v4.h, v4_prev)
                mmi = None
                for kc in range(KC):
                    blkap = l2t[sl][:, kc * 128:(kc + 1) * 128]
                    PE.matmul(ps_h[:, 0:32], blkap, gw1_sb[:, kc * 32:(kc + 1) * 32],
                              start=(kc == 0), stop=(kc == KC - 1))
                    mmi = PE.matmul(ps_tmp[:, 0:128], blkap,
                                    A_sb[:, kc * 128:(kc + 1) * 128],
                                    start=(kc == 0), stop=(kc == KC - 1))
                s_pe4.inc(mmi)
                s_l2[sl].inc(PE.nop())
                A.wait_ge(s_pe4.h, s_pe4.v)
                A.copy(h_sb[:, :], ps_h[:, 0:32])
                A.drain()
                s_a4.inc(A.copy(tmp_f[:, :], ps_tmp[:, 0:128]))
                V.wait_ge(s_a4.h, s_a4.v)
                V.tensor_tensor(h_sb[:, :], h_sb[:, :], gb1B[:, :], OP.add)
                V.drain()
                V.bn_stats(hg_st[:, :], h_sb[:, :])
                V.drain()
                V.bn_aggr(hg_mv[:, :], hg_st[:, :])
                V.drain()
                s_v4.inc(V.tensor_scalar(hg_nm[:, :], hg_mv[:, 0:1], -1.0, None, OP.mult))
                A.wait_ge(s_v4.h, s_v4.v)
                A.activation(hg_rs[:, :], hg_mv[:, 1:2], AF.Ln,
                             bias=eps_t[:, 0:1], scale=1.0)
                A.drain()
                s_a4.inc(A.activation(hg_rs[:, :], hg_rs[:, :], AF.Exp, scale=-0.5))
                V.wait_ge(s_a4.h, s_a4.v)
                V.tensor_tensor(hg_nmr[:, :], hg_nm[:, :], hg_rs[:, :], OP.mult)
                V.drain()
                V.tensor_scalar(hn_sb[:, :], h_sb[:, :], hg_rs[:, 0:1],
                                hg_nmr[:, 0:1], OP.mult, OP.add)
                V.drain()
                V.tensor_tensor(hn_sb[:, :], hn_sb[:, :], gamB[:, :], OP.mult)
                V.drain()
                V.tensor_tensor(hn_sb[:, :], hn_sb[:, :], betB[:, :], OP.add)
                V.drain()
                V.tensor_scalar(hn_sb[:, :], hn_sb[:, :], 0.0, None, OP.max)
                V.drain()
                s_v4.inc(V.tensor_copy(hb_sb[:, :], hn_sb[:, :]))
                PE.wait_ge(s_v4.h, s_v4.v)
                s_pe4.inc(PE.transpose(ps_hT[:, 0:128], hb_sb[:, :], ident[:, :]))
                A.wait_ge(s_pe4.h, s_pe4.v)
                s_a4.inc(A.copy(hT_sb[:, :], ps_hT[:, 0:128]))
                PE.wait_ge(s_a4.h, s_a4.v)
                s_pe4.inc(PE.matmul(ps_g[:, 0:8], hT_sb[:, :], gw2_sb[:, :],
                                    start=True, stop=True))
                A.wait_ge(s_pe4.h, s_pe4.v)
                s_a4.inc(A.copy(gates[:, :], ps_g[:, 0:8]))
                V.wait_ge(s_a4.h, s_a4.v)
                V.tensor_tensor(gates[:, :], gates[:, :], gb2B[:, :], OP.add)
                V.drain()
                V.reduce_max(m1[:, :], gates[:, :], axis=AX)
                V.drain()
                V.tensor_scalar(eq1f[:, :], gates[:, :], m1[:, 0:1], None, OP.is_equal)
                V.drain()
                V.tensor_copy(eqb[:, 0:8], eq1f[:, :])
                V.scalar_tensor_tensor(gates2[:, :], eq1f[:, :], BIG_NEG, gates[:, :],
                                       OP.mult, OP.add)
                V.drain()
                V.reduce_max(m2[:, :], gates2[:, :], axis=AX)
                V.drain()
                V.tensor_scalar(eq2f[:, :], gates2[:, :], m2[:, 0:1], None, OP.is_equal)
                V.drain()
                V.tensor_copy(eqb[:, 8:16], eq2f[:, :])
                s_v4.inc(V.scalar_tensor_tensor(dgap[:, :], m1[:, :], -1.0, m2[:, :],
                                                OP.mult, OP.add))
                A.wait_ge(s_v4.h, s_v4.v)
                s_a4.inc(A.activation(sexp[:, :], dgap[:, :], AF.Exp))
                V.wait_ge(s_a4.h, s_a4.v)
                V.tensor_scalar(sp1[:, :], sexp[:, :], 1.0, None, OP.add)
                V.drain()
                V.reciprocal(rs_[:, :], sp1[:, :])
                V.drain()
                V.tensor_tensor(p2[:, :], sexp[:, :], rs_[:, :], OP.mult)
                V.drain()
                V.tensor_scalar(p1[:, :], p2[:, :], -1.0, 1.0, OP.mult, OP.add)
                V.drain()
                V.tensor_scalar(g1[:, :], eq1f[:, :], p1[:, 0:1], ALPHA, OP.mult, OP.mult)
                V.drain()
                V.tensor_scalar(g2[:, :], eq2f[:, :], p2[:, 0:1], ALPHA, OP.mult, OP.mult)
                V.drain()
                g1v = g1[:, :].unsqueeze(2).broadcast_to([128, 8, 16])
                g2v = g2[:, :].unsqueeze(2).broadcast_to([128, 8, 16])
                tview = tmp_f[:, :].rearrange("p (ee r) -> p ee r", ee=8)
                V.tensor_tensor(tmp1b[:, :].rearrange("p (ee r) -> p ee r", ee=8),
                                tview, g1v, OP.mult)
                V.drain()
                s_v4.inc(V.tensor_tensor(
                    tmp2b[:, :].rearrange("p (ee r) -> p ee r", ee=8),
                    tview, g2v, OP.mult))
                PE.wait_ge(s_v4.h, s_v4.v)
                for tsrc, tdst in ((tmp1b, t1_sb), (tmp2b, t2_sb)):
                    if pt4 >= 2:
                        PE.wait_ge(s_tp4.h, pt4 - 1)
                    s_pe4.inc(PE.transpose(ps_tp4[pt4 % 2][:, 0:128],
                                           tsrc[:, :], ident[:, :]))
                    A.wait_ge(s_pe4.h, s_pe4.v)
                    s_tp4.inc(A.copy(tdst[:, tc * 128:(tc + 1) * 128],
                                     ps_tp4[pt4 % 2][:, 0:128]))
                    pt4 += 1
                s_pe4.inc(PE.matmul(ps_cnt[:, 0:1], eqb[:, :], ones_b[:, :],
                                    start=True, stop=True))
                A.wait_ge(s_pe4.h, s_pe4.v)
                s_a4.inc(A.copy(cnt_tmp[:, :], ps_cnt[:, 0:1]))
                V.wait_ge(s_a4.h, s_a4.v)
                s_v4.inc(V.tensor_tensor(cnt_acc[:, :], cnt_acc[:, :],
                                         cnt_tmp[:, :], OP.add))
                V.drain()
                v4_prev = s_v4.v

            SC.wait_ge(s_v4.h, s_v4.v)
            SC.wait_ge(s_tp4.h, s_tp4.v)
            s_out.inc(SC.dma_start(out=cnt_d, in_=cnt_acc[:, :]), 16)
            s_out.inc(SC.dma_start(out=t1_d, in_=t1_sb[:, :]), 16)
            s_out.inc(SC.dma_start(out=t2_d, in_=t2_sb[:, :]), 16)
            for sem in s_x2s + s_l2:
                SC.wait_ge(sem.h, sem.v)
            SC.wait_ge(s_out.h, s_out.v)
        es_bst.close()

    return nc



def build_B():
    nc = bass.Bass("TRN2", target_bir_lowering=False, debug=False)

    cnts_d = nc.dram_tensor("cnts", [16, 8], F32, kind="ExternalInput").ap()
    x2_d = nc.dram_tensor("x2i", [TOK, D], F32, kind="ExternalInput").ap()
    l2t_d = nc.dram_tensor("l2ti", [TOK, D], BF16, kind="ExternalInput").ap()
    t1_d = nc.dram_tensor("t1i", [128, TOK], BF16, kind="ExternalInput").ap()
    t2_d = nc.dram_tensor("t2i", [128, TOK], BF16, kind="ExternalInput").ap()
    wfT_d = nc.dram_tensor("wfT", [D, D], F32, kind="ExternalInput").ap()
    bRe_d = nc.dram_tensor("bRe", [128, D], F32, kind="ExternalInput").ap()

    out_d = nc.dram_tensor("out", [TOK, D], F32, kind="ExternalOutput").ap()
    e_dram = nc.dram_tensor("e_scr", [TOK, D], BF16)

    PE, V, A, G, SY, SC = nc.tensor, nc.vector, nc.scalar, nc.gpsimd, nc.sync, nc.scalar

    with ExitStack() as es:
        e = es.enter_context

        eps_t = e(nc.sbuf_tensor("eps_t", [128, 1], F32))
        ea0 = e(nc.sbuf_tensor("ea0", [16, 128], BF16))
        ea1 = e(nc.sbuf_tensor("ea1", [16, 128], BF16))
        cnts = e(nc.sbuf_tensor("cnts_s", [16, 8], F32))
        tot = e(nc.sbuf_tensor("tot", [16, 1], F32))
        alw = e(nc.sbuf_tensor("alw", [16, 1], F32))
        alwb = e(nc.sbuf_tensor("alwb", [16, 1], BF16))
        a0x = e(nc.sbuf_tensor("a0x", [128, 1], F32))
        a1x = e(nc.sbuf_tensor("a1x", [128, 1], F32))
        estats = e(nc.sbuf_tensor("estats", [128, TC * NTN * 6], F32))
        stats = [e(nc.sbuf_tensor(f"st{i}", [128, 96], F32)) for i in range(2)]
        mv = [e(nc.sbuf_tensor(f"mv{i}", [128, 2], F32)) for i in range(2)]
        nm = [e(nc.sbuf_tensor(f"nm{i}", [128, 1], F32)) for i in range(2)]
        nmr = [e(nc.sbuf_tensor(f"nmr{i}", [128, 1], F32)) for i in range(2)]
        rstd = [e(nc.sbuf_tensor(f"rs{i}", [128, 1], F32)) for i in range(2)]

        s_ld = Sem(es, nc, "s_ld")
        s_q1 = Sem(es, nc, "s_q1")
        s_q1a = Sem(es, nc, "s_q1a")
        s_q1p = Sem(es, nc, "s_q1p")
        s_l2 = Sem(es, nc, "s_l2")
        s_wf = [Sem(es, nc, f"s_wf{i}") for i in range(2)]
        s_wff = [Sem(es, nc, f"s_wff{i}") for i in range(2)]
        s_mm = Sem(es, nc, "s_mm")
        s_aev = Sem(es, nc, "s_aev")
        s_stg = [Sem(es, nc, f"s_stg{i}") for i in range(8)]
        s_pe = [Sem(es, nc, f"s_pe{i}") for i in range(2)]
        s_pxx = [Sem(es, nc, f"s_pxx{i}") for i in range(2)]
        s_v3 = Sem(es, nc, "s_v3")
        s_a3 = Sem(es, nc, "s_a3")
        s_p3 = Sem(es, nc, "s_p3")
        s_out = Sem(es, nc, "s_out")

        blk = e(nc.Block())

        es_q = ExitStack()
        t1 = es_q.enter_context(nc.sbuf_tensor("t1", [128, TOK], BF16))
        t2 = es_q.enter_context(nc.sbuf_tensor("t2", [128, TOK], BF16))
        tall = es_q.enter_context(nc.sbuf_tensor("tall", [128, TOK], BF16))
        b_sb = es_q.enter_context(nc.sbuf_tensor("b_sb", [128, D], BF16))
        l2T = es_q.enter_context(nc.sbuf_tensor("l2T", [128, KC * TOK], BF16))

        V.memset(eps_t[:, :], EPS)
        G.memset(ea0[:, :], 1.0)
        G.affine_select(out=ea0[:, :], in_=ea0[:, :], compare_op=OP.is_ge,
                        fill=0.0, base=0, pattern=[[1, 128]], channel_multiplier=-16)
        G.affine_select(out=ea0[:, :], in_=ea0[:, :], compare_op=OP.is_ge,
                        fill=0.0, base=15, pattern=[[-1, 128]], channel_multiplier=16)
        G.memset(ea1[:, :], 1.0)
        G.affine_select(out=ea1[:, :], in_=ea1[:, :], compare_op=OP.is_ge,
                        fill=0.0, base=128, pattern=[[1, 128]], channel_multiplier=-16)
        s_q1p.inc(G.affine_select(out=ea1[:, :], in_=ea1[:, :], compare_op=OP.is_ge,
                                  fill=0.0, base=-113, pattern=[[-1, 128]],
                                  channel_multiplier=16))
        s_ld.inc(SY.dma_start(out=cnts[:, :], in_=cnts_d), 16)
        s_ld.inc(SY.dma_start(out=t1[:, :], in_=t1_d), 16)
        s_ld.inc(SY.dma_start(out=t2[:, :], in_=t2_d), 16)
        s_q1.inc(G.dma_start(out=b_sb[:, :], in_=bRe_d), 16)

        V.wait_ge(s_ld.h, s_ld.v)
        V.tensor_reduce(tot[:, :], cnts[:, :], axis=AX, op=OP.add)
        V.drain()
        V.tensor_scalar(alw[:, :], tot[:, :], CAPACITY + 0.5, None, OP.is_lt)
        V.drain()
        s_v3.inc(V.tensor_copy(alwb[:, :], alw[:, :]))
        with ExitStack() as esq:
            ps_e = esq.enter_context(nc.psum_tensor("ps_e", [128, 512], F32))
            PE.wait_ge(s_v3.h, s_v3.v)
            PE.wait_ge(s_q1p.h, s_q1p.v)
            s_q1.inc(PE.matmul(ps_e[:, 0:1], ea0[:, :], alwb[:, :],
                               start=True, stop=True), 1)
            A.wait_ge(s_q1.h, s_q1.v)
            s_q1a.inc(A.copy(a0x[:, :], ps_e[:, 0:1]))
            PE.wait_ge(s_q1a.h, s_q1a.v)
            s_q1.inc(PE.matmul(ps_e[:, 0:1], ea1[:, :], alwb[:, :],
                               start=True, stop=True), 1)
            A.wait_ge(s_q1.h, s_q1.v)
            s_q1a.inc(A.copy(a1x[:, :], ps_e[:, 0:1]))
        V.wait_ge(s_q1a.h, s_q1a.v)
        V.tensor_scalar(tall[:, :], t2[:, :], a1x[:, 0:1], None, OP.mult)
        V.drain()
        s_v3.inc(V.scalar_tensor_tensor(tall[:, :], t1[:, :], a0x[:, 0:1],
                                        tall[:, :], OP.mult, OP.add))

        l2Tv = l2T[:, :].rearrange("p (kc t) -> p kc t", kc=KC)
        for tc in range(TC):
            s_l2.inc(SY.dma_start(
                out=l2Tv[:, :, tc * 128:(tc + 1) * 128],
                in_=l2t_d[tc * 128:(tc + 1) * 128, :]
                    .rearrange("p (kc t) -> p kc t", kc=KC)), 16)

        with ExitStack() as es3:
            wf = [es3.enter_context(nc.sbuf_tensor(f"wf{i}", [128, KC * NTW], BF16))
                  for i in range(2)]
            stage = [es3.enter_context(nc.sbuf_tensor(f"stg{i}", [128, NTW], BF16))
                     for i in range(8)]
            ps_acc = [es3.enter_context(nc.psum_tensor(f"ps_acc{i}", [128, 512], F32))
                      for i in range(8)]

            grp = 0
            for nt in range(NTN):
                sl = nt % 2
                if nt >= 2:
                    G.wait_ge(s_wff[sl].h, s_wff[sl].v)
                s_wf[sl].inc(G.dma_start(
                    out=wf[sl][:, :].rearrange("p (kc n) -> p kc n", kc=KC),
                    in_=wfT_d[:, nt * NTW:(nt + 1) * NTW]
                        .rearrange("(kc kp) n -> kp kc n", kp=128)), 16)
                PE.wait_ge(s_wf[sl].h, s_wf[sl].v)
                if nt == 0:
                    PE.wait_ge(s_l2.h, s_l2.v)
                    PE.wait_ge(s_v3.h, s_v3.v)
                    PE.wait_ge(s_q1.h, s_q1.v)
                    PE.wait_ge(s_q1a.h, s_q1a.v)
                for tg in range(TG):
                    bank = (grp % 2) * 4
                    if grp >= 2:
                        PE.wait_ge(s_aev.h, 4 * (grp - 1))
                    for kc in range(KC):
                        for tci in range(4):
                            tc = tg * 4 + tci
                            PE.matmul(
                                ps_acc[bank + tci][:, 0:NTW],
                                l2T[:, kc * TOK + tc * 128:kc * TOK + (tc + 1) * 128],
                                wf[sl][:, kc * NTW:(kc + 1) * NTW],
                                start=(kc == 0), stop=False)
                    mmi = None
                    for tci in range(4):
                        tc = tg * 4 + tci
                        mmi = PE.matmul(ps_acc[bank + tci][:, 0:NTW],
                                        tall[:, tc * 128:(tc + 1) * 128],
                                        b_sb[:, nt * NTW:(nt + 1) * NTW],
                                        start=False, stop=True)
                    s_mm.inc(mmi)
                    if tg == TG - 1:
                        s_wff[sl].inc(PE.nop())
                    A.wait_ge(s_mm.h, s_mm.v)
                    for tci in range(4):
                        tc = tg * 4 + tci
                        ssl = (grp % 2) * 4 + tci
                        uses = grp // 2
                        if uses >= 1:
                            A.wait_ge(s_stg[ssl].h, 17 * uses)
                        s_aev.inc(A.copy(stage[ssl][:, :], ps_acc[bank + tci][:, 0:NTW]))
                        SC.wait_ge(s_aev.h, s_aev.v)
                        s_stg[ssl].inc(SC.dma_start(
                            out=e_dram.ap()[tc * 128:(tc + 1) * 128,
                                            nt * NTW:(nt + 1) * NTW],
                            in_=stage[ssl][:, :]), 16)
                        V.wait_ge(s_aev.h, s_aev.v)
                        s_stg[ssl].inc(V.bn_stats(
                            estats[:, (tc * NTN + nt) * 6:(tc * NTN + nt) * 6 + 6],
                            stage[ssl][:, :]))
                    grp += 1
        es_q.close()

        with ExitStack() as es4:
            ebuf = [es4.enter_context(nc.sbuf_tensor(f"eb{i}", [128, D], BF16))
                    for i in range(2)]
            x2b = [es4.enter_context(nc.sbuf_tensor(f"x2b{i}", [128, D], F32))
                   for i in range(2)]
            f_sb = es4.enter_context(nc.sbuf_tensor("f_sb", [128, D], F32))
            o_sb = [es4.enter_context(nc.sbuf_tensor(f"o_sb{i}", [128, D], F32))
                    for i in range(2)]

            SY.wait_ge(s_aev.h, s_aev.v)
            for i in range(8):
                SY.wait_ge(s_stg[i].h, s_stg[i].v)
            for tc in range(TC):
                sl = tc % 2
                if tc >= 2:
                    SY.wait_ge(s_p3.h, s_p3.v)
                    SY.wait_ge(s_v3.h, s_v3.v)
                s_pe[sl].inc(SY.dma_start(
                    out=ebuf[sl][:, :],
                    in_=e_dram.ap()[tc * 128:(tc + 1) * 128, :]), 16)
                s_pxx[sl].inc(SY.dma_start(
                    out=x2b[sl][:, :], in_=x2_d[tc * 128:(tc + 1) * 128, :]), 16)
                V.wait_ge(s_pe[sl].h, s_pe[sl].v)
                V.bn_aggr(mv[sl][:, :], estats[:, tc * 96:(tc + 1) * 96])
                V.drain()
                s_v3.inc(V.tensor_scalar(nm[sl][:, :], mv[sl][:, 0:1], -1.0, None, OP.mult))
                A.wait_ge(s_v3.h, s_v3.v)
                A.activation(rstd[sl][:, :], mv[sl][:, 1:2], AF.Ln,
                             bias=eps_t[:, 0:1], scale=1.0)
                A.drain()
                s_a3.inc(A.activation(rstd[sl][:, :], rstd[sl][:, :], AF.Exp, scale=-0.5))
                V.wait_ge(s_a3.h, s_a3.v)
                V.tensor_tensor(nmr[sl][:, :], nm[sl][:, :], rstd[sl][:, :], OP.mult)
                V.drain()
                V.wait_ge(s_p3.h, s_p3.v)
                s_v3.inc(V.tensor_scalar(f_sb[:, :], ebuf[sl][:, :],
                                         rstd[sl][:, 0:1], nmr[sl][:, 0:1],
                                         OP.mult, OP.add))
                G.wait_ge(s_v3.h, s_v3.v)
                G.wait_ge(s_pxx[sl].h, s_pxx[sl].v)
                if tc >= 2:
                    G.wait_ge(s_out.h, s_out.v - 16)
                s_p3.inc(G.tensor_tensor(o_sb[sl][:, :], x2b[sl][:, :],
                                         f_sb[:, :], OP.add))
                SC.wait_ge(s_p3.h, s_p3.v)
                s_out.inc(SC.dma_start(
                    out=out_d[tc * 128:(tc + 1) * 128, :], in_=o_sb[sl][:, :]), 16)
            SC.wait_ge(s_out.h, s_out.v)

    return nc



def kernel(L0_attn_data, L0_attn_weight, L0_attn_mask, L0_ffn_weight,
           L0_ffn_lora_A, L0_ffn_lora_B, L0_ffn_gw1, L0_ffn_gb1,
           L0_ffn_ln_gamma, L0_ffn_ln_beta, L0_ffn_gw2, L0_ffn_gb2):
    x = np.asarray(L0_attn_data, dtype=np.float32)
    waT = np.ascontiguousarray(np.asarray(L0_attn_weight, np.float32).T)
    wfT = np.ascontiguousarray(np.asarray(L0_ffn_weight, np.float32).T)
    mask = np.asarray(L0_attn_mask, np.int32)
    maskR = np.ascontiguousarray(
        mask.reshape(128, 32, 32, 4, 32).transpose(3, 2, 0, 4, 1).reshape(128, 128, 1024))
    aRe = np.ascontiguousarray(
        np.asarray(L0_ffn_lora_A, np.float32).transpose(1, 0, 2).reshape(D, 128))
    bRe = np.ascontiguousarray(np.asarray(L0_ffn_lora_B, np.float32).reshape(128, D))
    gw1T = np.ascontiguousarray(np.asarray(L0_ffn_gw1, np.float32).T)
    gw2T = np.ascontiguousarray(np.asarray(L0_ffn_gw2, np.float32).T)
    gb1B = np.ascontiguousarray(np.broadcast_to(
        np.asarray(L0_ffn_gb1, np.float32), (128, 32)))
    gamB = np.ascontiguousarray(np.broadcast_to(
        np.asarray(L0_ffn_ln_gamma, np.float32), (128, 32)))
    betB = np.ascontiguousarray(np.broadcast_to(
        np.asarray(L0_ffn_ln_beta, np.float32), (128, 32)))
    gb2B = np.ascontiguousarray(np.broadcast_to(
        np.asarray(L0_ffn_gb2, np.float32), (128, 8)))

    ncA = build_A()
    in_maps_A = [{
        "x": np.ascontiguousarray(x[c]),
        "waT": waT, "maskR": maskR, "aRe": aRe,
        "gw1T": gw1T, "gw2T": gw2T,
        "gb1B": gb1B, "gamB": gamB, "betB": betB, "gb2B": gb2B,
    } for c in range(NCORES)]
    resA = run_bass_kernel_spmd(ncA, in_maps_A, core_ids=list(range(NCORES)))

    cnts = np.ascontiguousarray(
        np.concatenate([resA.results[c]["counts"] for c in range(NCORES)], axis=1))

    ncB = build_B()
    in_maps_B = [{
        "cnts": cnts,
        "x2i": resA.results[c]["x2o"],
        "l2ti": resA.results[c]["l2to"],
        "t1i": resA.results[c]["t1o"],
        "t2i": resA.results[c]["t2o"],
        "wfT": wfT, "bRe": bRe,
    } for c in range(NCORES)]
    resB = run_bass_kernel_spmd(ncB, in_maps_B, core_ids=list(range(NCORES)))

    out = np.stack([resB.results[c]["out"] for c in range(NCORES)], axis=0)
    return np.ascontiguousarray(out, dtype=np.float32)


# version 03
# speedup vs baseline: 1.0524x; 1.0524x over previous
import numpy as np
from contextlib import ExitStack

import concourse.bass as bass
import concourse.mybir as mybir
from concourse.bass_utils import run_bass_kernel_spmd
from concourse.alu_op_type import AluOpType as OP

F32 = mybir.dt.float32
BF16 = mybir.dt.bfloat16
I32 = mybir.dt.int32
AF = mybir.ActivationFunctionType
AX = mybir.AxisListType.X

NCORES = 8
TOK = 2048
TC = 16
D = 4096
KC = 32
KH = 16
NTN = 8
NTW = 512
TG = 4
NWS = 3
EPS = 1e-5
CAPACITY = float(int(3.0 * (NCORES * TOK) / 8))
BIG_NEG = -1.0e30
ALPHA = 1.0 / 16.0


class Sem:

    def __init__(self, es, nc, name):
        self.h = es.enter_context(nc.semaphore(name))
        self.v = 0

    def inc(self, instr, k=1):
        instr.then_inc(self.h, k)
        self.v += k
        return instr


def ln_chain(V, A, x_ap, out_ap, st, mv, nm, nmr, rstd, eps_t, sv, sa, nstats):
    w = D // nstats
    for i in range(nstats):
        V.bn_stats(st[:, 6 * i:6 * i + 6], x_ap[:, w * i:w * (i + 1)])
    V.drain()
    V.bn_aggr(mv[:, :], st[:, 0:6 * nstats])
    V.drain()
    sv.inc(V.tensor_scalar(nm[:, :], mv[:, 0:1], -1.0, None, OP.mult))
    A.wait_ge(sv.h, sv.v)
    A.activation(rstd[:, :], mv[:, 1:2], AF.Ln, bias=eps_t[:, 0:1], scale=1.0)
    A.drain()
    sa.inc(A.activation(rstd[:, :], rstd[:, :], AF.Exp, scale=-0.5))
    V.wait_ge(sa.h, sa.v)
    V.tensor_tensor(nmr[:, :], nm[:, :], rstd[:, :], OP.mult)
    V.drain()
    return V.tensor_scalar(out_ap, x_ap, rstd[:, 0:1], nmr[:, 0:1], OP.mult, OP.add)



def build_A():
    nc = bass.Bass("TRN2", target_bir_lowering=False, debug=False)

    x_d = nc.dram_tensor("x", [TOK, D], F32, kind="ExternalInput").ap()
    waT_d = nc.dram_tensor("waT", [D, D], F32, kind="ExternalInput").ap()
    maskR_d = nc.dram_tensor("maskR", [128, 128, 1024], I32, kind="ExternalInput").ap()
    aRe_d = nc.dram_tensor("aRe", [D, 128], F32, kind="ExternalInput").ap()
    gw1T_d = nc.dram_tensor("gw1T", [D, 32], F32, kind="ExternalInput").ap()
    gw2T_d = nc.dram_tensor("gw2T", [32, 8], F32, kind="ExternalInput").ap()
    gb1B_d = nc.dram_tensor("gb1B", [128, 32], F32, kind="ExternalInput").ap()
    gamB_d = nc.dram_tensor("gamB", [128, 32], F32, kind="ExternalInput").ap()
    betB_d = nc.dram_tensor("betB", [128, 32], F32, kind="ExternalInput").ap()
    gb2B_d = nc.dram_tensor("gb2B", [128, 8], F32, kind="ExternalInput").ap()

    cnt_d = nc.dram_tensor("counts", [16, 1], F32, kind="ExternalOutput").ap()
    x2_d = nc.dram_tensor("x2o", [TOK, D], F32, kind="ExternalOutput").ap()
    l2t_d = nc.dram_tensor("l2to", [TOK, D], BF16, kind="ExternalOutput").ap()
    t1_d = nc.dram_tensor("t1o", [128, TOK], BF16, kind="ExternalOutput").ap()
    t2_d = nc.dram_tensor("t2o", [128, TOK], BF16, kind="ExternalOutput").ap()

    b_dram = nc.dram_tensor("b_scr", [TOK, D], BF16)
    bin_dram = nc.dram_tensor("bin_scr", [128, 128], BF16)

    PE, V, A, G, SY, SC = nc.tensor, nc.vector, nc.scalar, nc.gpsimd, nc.sync, nc.scalar

    with ExitStack() as es:
        e = es.enter_context

        ident = e(nc.sbuf_tensor("ident", [128, 128], BF16))
        e4t = e(nc.sbuf_tensor("e4t", [4, 128], BF16))
        eps_t = e(nc.sbuf_tensor("eps_t", [128, 1], F32))
        ones_b = e(nc.sbuf_tensor("ones_b", [128, 1], BF16))
        gw2_sb = e(nc.sbuf_tensor("gw2_sb", [32, 8], BF16))
        gb1B = e(nc.sbuf_tensor("gb1B_s", [128, 32], F32))
        gamB = e(nc.sbuf_tensor("gamB_s", [128, 32], F32))
        betB = e(nc.sbuf_tensor("betB_s", [128, 32], F32))
        gb2B = e(nc.sbuf_tensor("gb2B_s", [128, 8], F32))
        bs_all = e(nc.sbuf_tensor("bs_all", [128, 128], F32))
        binX = e(nc.sbuf_tensor("binX", [128, 128], BF16))
        cnt_acc = e(nc.sbuf_tensor("cnt_acc", [16, 1], F32))
        cnt_tmp = e(nc.sbuf_tensor("cnt_tmp", [16, 1], F32))
        stats = [e(nc.sbuf_tensor(f"st{i}", [128, 96], F32)) for i in range(2)]
        mv = [e(nc.sbuf_tensor(f"mv{i}", [128, 2], F32)) for i in range(2)]
        nm = [e(nc.sbuf_tensor(f"nm{i}", [128, 1], F32)) for i in range(2)]
        nmr = [e(nc.sbuf_tensor(f"nmr{i}", [128, 1], F32)) for i in range(2)]
        rstd = [e(nc.sbuf_tensor(f"rs{i}", [128, 1], F32)) for i in range(2)]
        h_sb = e(nc.sbuf_tensor("h_sb", [128, 32], F32))
        hn_sb = e(nc.sbuf_tensor("hn_sb", [128, 32], F32))
        hb_sb = e(nc.sbuf_tensor("hb_sb", [128, 32], BF16))
        hT_sb = e(nc.sbuf_tensor("hT_sb", [32, 128], BF16))
        gates = e(nc.sbuf_tensor("gates", [128, 8], F32))
        gates2 = e(nc.sbuf_tensor("gates2", [128, 8], F32))
        m1 = e(nc.sbuf_tensor("m1", [128, 1], F32))
        m2 = e(nc.sbuf_tensor("m2", [128, 1], F32))
        eqb = e(nc.sbuf_tensor("eqb", [128, 16], BF16))
        eq1f = e(nc.sbuf_tensor("eq1f", [128, 8], F32))
        eq2f = e(nc.sbuf_tensor("eq2f", [128, 8], F32))
        dgap = e(nc.sbuf_tensor("dgap", [128, 1], F32))
        sexp = e(nc.sbuf_tensor("sexp", [128, 1], F32))
        sp1 = e(nc.sbuf_tensor("sp1", [128, 1], F32))
        rs_ = e(nc.sbuf_tensor("rs_", [128, 1], F32))
        p1 = e(nc.sbuf_tensor("p1", [128, 1], F32))
        p2 = e(nc.sbuf_tensor("p2", [128, 1], F32))
        g1 = e(nc.sbuf_tensor("g1", [128, 8], F32))
        g2 = e(nc.sbuf_tensor("g2", [128, 8], F32))
        tmp_f = e(nc.sbuf_tensor("tmp_f", [128, 128], F32))
        tmp1b = e(nc.sbuf_tensor("tmp1b", [128, 128], BF16))
        tmp2b = e(nc.sbuf_tensor("tmp2b", [128, 128], BF16))
        hg_st = e(nc.sbuf_tensor("hg_st", [128, 6], F32))
        hg_mv = e(nc.sbuf_tensor("hg_mv", [128, 2], F32))
        hg_nm = e(nc.sbuf_tensor("hg_nm", [128, 1], F32))
        hg_nmr = e(nc.sbuf_tensor("hg_nmr", [128, 1], F32))
        hg_rs = e(nc.sbuf_tensor("hg_rs", [128, 1], F32))

        s_const = Sem(es, nc, "s_const")
        s_ld1 = Sem(es, nc, "s_ld1")
        s_msk = [Sem(es, nc, f"s_msk{i}") for i in range(2)]
        s_bin = Sem(es, nc, "s_bin")
        s_binp = Sem(es, nc, "s_binp")
        s_binpm = Sem(es, nc, "s_binpm")
        s_ldx = [Sem(es, nc, f"s_ldx{i}") for i in range(2)]
        s_abf = [Sem(es, nc, f"s_abf{i}") for i in range(2)]
        s_aused = Sem(es, nc, "s_aused")
        s_lnv = Sem(es, nc, "s_lnv")
        s_lna = Sem(es, nc, "s_lna")
        s_pt = Sem(es, nc, "s_pt")
        s_ev = Sem(es, nc, "s_ev")
        s_wm = [Sem(es, nc, f"s_wm{i}") for i in range(NWS)]
        s_wmf = [Sem(es, nc, f"s_wmf{i}") for i in range(NWS)]
        s_wa = [Sem(es, nc, f"s_wa{i}") for i in range(NWS)]
        s_mm = Sem(es, nc, "s_mm")
        s_aev = Sem(es, nc, "s_aev")
        s_stg = [Sem(es, nc, f"s_stg{i}") for i in range(8)]
        s_pb = [Sem(es, nc, f"s_pb{i}") for i in range(2)]
        s_px = [Sem(es, nc, f"s_px{i}") for i in range(2)]
        s_v4 = Sem(es, nc, "s_v4")
        s_a4 = Sem(es, nc, "s_a4")
        s_p4 = Sem(es, nc, "s_p4")
        s_pe4 = Sem(es, nc, "s_pe4")
        s_tp4 = Sem(es, nc, "s_tp4")
        s_l2u = Sem(es, nc, "s_l2u")
        s_x2s = [Sem(es, nc, f"s_x2s{i}") for i in range(2)]
        s_l2 = [Sem(es, nc, f"s_l2{i}") for i in range(2)]
        s_out = Sem(es, nc, "s_out")

        blk = e(nc.Block())

        G.memset(ident[:, :], 0.0)
        G.affine_select(out=ident[:, :], in_=ident[:, :], compare_op=OP.not_equal,
                        fill=1.0, base=0, pattern=[[-1, 128]], channel_multiplier=1)
        G.memset(e4t[:, :], 1.0)
        G.affine_select(out=e4t[:, :], in_=e4t[:, :], compare_op=OP.is_ge,
                        fill=0.0, base=0, pattern=[[1, 128]], channel_multiplier=-32)
        G.affine_select(out=e4t[:, :], in_=e4t[:, :], compare_op=OP.is_ge,
                        fill=0.0, base=31, pattern=[[-1, 128]], channel_multiplier=32)
        s_const.inc(G.memset(ones_b[:, :], 1.0))
        s_const.inc(G.dma_start(out=gw2_sb[:, :], in_=gw2T_d), 16)
        s_ld1.inc(SY.dma_start(out=gb1B[:, :], in_=gb1B_d), 16)
        s_ld1.inc(SY.dma_start(out=gamB[:, :], in_=gamB_d), 16)
        s_ld1.inc(SY.dma_start(out=betB[:, :], in_=betB_d), 16)
        s_ld1.inc(SY.dma_start(out=gb2B[:, :], in_=gb2B_d), 16)
        V.memset(eps_t[:, :], EPS)
        V.memset(cnt_acc[:, :], 0.0)
        V.drain()

        es_big = ExitStack()
        aT = es_big.enter_context(nc.sbuf_tensor("aT", [128, KC * TOK], BF16))
        binP = es_big.enter_context(nc.sbuf_tensor("binP", [128, 4096], BF16))
        es_bst = ExitStack()

        es_msk = ExitStack()
        msk = [es_msk.enter_context(nc.sbuf_tensor(f"msk{i}", [128, 4096], BF16))
               for i in range(2)]

        with ExitStack() as es2:
            xbuf = [es2.enter_context(nc.sbuf_tensor(f"xb{i}", [128, D], F32))
                    for i in range(2)]
            abf = [es2.enter_context(nc.sbuf_tensor(f"ab{i}", [128, D], BF16))
                   for i in range(2)]
            ps_tp = [es2.enter_context(nc.psum_tensor(f"ps_tp{i}", [128, 1024], BF16))
                     for i in range(2)]

            aTv = aT[:, :].rearrange("p (kc t) -> p kc t", kc=KC)
            for tc in range(TC):
                sl = tc % 2
                if tc >= 2:
                    SY.wait_ge(s_abf[sl].h, s_abf[sl].v)
                s_ldx[sl].inc(SY.dma_start(
                    out=xbuf[sl][:, :], in_=x_d[tc * 128:(tc + 1) * 128, :]), 16)
                V.wait_ge(s_ldx[sl].h, s_ldx[sl].v)
                if tc >= 2:
                    V.wait_ge(s_aused.h, tc - 1)
                ap = ln_chain(V, A, xbuf[sl][:, :], abf[sl][:, :], stats[sl], mv[sl],
                              nm[sl], nmr[sl], rstd[sl], eps_t, s_lnv, s_lna, 8)
                s_abf[sl].inc(ap)
                if tc == 0:
                    PE.wait_ge(s_const.h, s_const.v)
                PE.wait_ge(s_abf[sl].h, s_abf[sl].v)
                for kc in range(KC):
                    j = tc * KC + kc
                    bank = (j // 8) % 2
                    slot = j % 8
                    if j >= 16 and slot == 0:
                        PE.wait_ge(s_ev.h, j // 8 - 1)
                    s_pt.inc(PE.transpose(ps_tp[bank][:, slot * 128:(slot + 1) * 128],
                                          abf[sl][:, kc * 128:(kc + 1) * 128],
                                          ident[:, :]))
                    if kc == KC - 1:
                        s_aused.inc(PE.nop())
                    if slot == 7:
                        A.wait_ge(s_pt.h, s_pt.v)
                        s_ev.inc(A.copy(aTv[:, kc - 7:kc + 1, tc * 128:(tc + 1) * 128],
                                        ps_tp[bank][:, :].rearrange(
                                            "p (s t) -> p s t", s=8)))
                for mc in (2 * tc, 2 * tc + 1):
                    msl = mc % 2
                    if mc >= 2:
                        G.wait_ge(s_msk[msl].h, s_msk[msl].v)
                    s_msk[msl].inc(G.dma_start(
                        out=msk[msl][:, :].rearrange("p (a j) -> p a j", a=4),
                        in_=maskR_d[:, 4 * mc:4 * (mc + 1), :]), 16)
                    V.wait_ge(s_msk[msl].h, s_msk[msl].v)
                    s_msk[msl].inc(V.tensor_reduce(
                        bs_all[:, 4 * mc:4 * (mc + 1)],
                        msk[msl][:, :].rearrange("p (a j) -> p a j", a=4),
                        axis=AX, op=OP.add))
                    V.drain()

        V.wait_ge(s_msk[0].h, s_msk[0].v)
        V.wait_ge(s_msk[1].h, s_msk[1].v)
        s_bin.inc(V.tensor_scalar(binX[:, :], bs_all[:, :], 0.0, None, OP.is_gt))
        SC.wait_ge(s_bin.h, s_bin.v)
        s_bin.inc(SC.dma_start(out=bin_dram.ap(), in_=binX[:, :]), 16)
        with ExitStack() as esb:
            bin_g = esb.enter_context(nc.sbuf_tensor("bin_g", [4, 4096], BF16))
            ps_bin = [esb.enter_context(nc.psum_tensor(f"ps_bin{i}", [128, 512], F32))
                      for i in range(2)]
            SY.wait_ge(s_bin.h, s_bin.v)
            s_bin.inc(SY.dma_start(
                out=bin_g[:, :].rearrange("g (kc nb) -> g kc nb", kc=KC),
                in_=bin_dram.ap().rearrange("(g kc) nb -> g kc nb", g=4)), 16)
            PE.wait_ge(s_bin.h, s_bin.v)
            bgv = bin_g[:, :].rearrange("g (kc nb) -> g kc nb", kc=KC)
            for nt in range(NTN):
                if nt >= 2:
                    PE.wait_ge(s_binp.h, nt - 1)
                s_binpm.inc(PE.matmul(ps_bin[nt % 2][:, 0:512], e4t[:, :],
                                      bgv[:, :, nt * 16:(nt + 1) * 16],
                                      start=True, stop=True))
                A.wait_ge(s_binpm.h, s_binpm.v)
                s_binp.inc(A.copy(binP[:, nt * 512:(nt + 1) * 512],
                                  ps_bin[nt % 2][:, 0:512]))
        es_msk.close()

        bstats = es_bst.enter_context(
            nc.sbuf_tensor("bstats", [128, TC * NTN * 6], F32, side="right"))
        with ExitStack() as es3:
            wm = [es3.enter_context(nc.sbuf_tensor(f"wm{i}", [128, KH * NTW], BF16))
                  for i in range(NWS)]
            stage = [es3.enter_context(nc.sbuf_tensor(f"stg{i}", [128, NTW], BF16))
                     for i in range(8)]
            ps_acc = [es3.enter_context(nc.psum_tensor(f"ps_acc{i}", [128, 512], F32))
                      for i in range(8)]

            grp = 0
            for nt in range(NTN):
                for kh in range(2):
                    sidx = nt * 2 + kh
                    wsl = sidx % NWS
                    if sidx == 0:
                        G.wait_ge(s_binp.h, s_binp.v)
                        G.wait_ge(s_msk[0].h, s_msk[0].v)
                        G.wait_ge(s_msk[1].h, s_msk[1].v)
                    if sidx >= NWS:
                        G.wait_ge(s_wmf[wsl].h, s_wmf[wsl].v)
                    s_wa[wsl].inc(G.dma_start(
                        out=wm[wsl][:, :].rearrange("p (kc n) -> p kc n", kc=KH),
                        in_=waT_d[kh * KH * 128:(kh + 1) * KH * 128,
                                  nt * NTW:(nt + 1) * NTW]
                            .rearrange("(kc kp) n -> kp kc n", kp=128)), 16)
                    APL = V if sidx % 2 == 0 else G
                    APL.wait_ge(s_wa[wsl].h, s_wa[wsl].v)
                    APL.wait_ge(s_binp.h, s_binp.v)
                    bview = (binP[:, nt * 512 + kh * 256:nt * 512 + (kh + 1) * 256]
                             .rearrange("p (kc nb) -> p kc nb", kc=KH)
                             .unsqueeze(3).broadcast_to([128, KH, 16, 32]))
                    wview = wm[wsl][:, :].rearrange("p (kc nb n) -> p kc nb n",
                                                    kc=KH, nb=16)
                    s_wm[wsl].inc(APL.tensor_tensor(wview, wview, bview, OP.mult))

                for tg in range(TG):
                    bank = (grp % 2) * 4
                    if grp == 0:
                        PE.wait_ge(s_ev.h, s_ev.v)
                        PE.wait_ge(s_binp.h, s_binp.v)
                    if grp >= 2:
                        PE.wait_ge(s_aev.h, 4 * (grp - 1))
                    mmi = None
                    for kc in range(KC):
                        wsl = (nt * 2 + kc // KH) % NWS
                        if kc % KH == 0:
                            PE.wait_ge(s_wm[wsl].h, s_wm[wsl].v)
                        for tci in range(4):
                            tc = tg * 4 + tci
                            mmi = PE.matmul(
                                ps_acc[bank + tci][:, 0:NTW],
                                aT[:, kc * TOK + tc * 128:kc * TOK + (tc + 1) * 128],
                                wm[wsl][:, (kc % KH) * NTW:(kc % KH + 1) * NTW],
                                start=(kc == 0), stop=(kc == KC - 1))
                    s_mm.inc(mmi)
                    if tg == TG - 1:
                        s_wmf[(nt * 2) % NWS].inc(PE.nop())
                        s_wmf[(nt * 2 + 1) % NWS].inc(PE.nop())
                    A.wait_ge(s_mm.h, s_mm.v)
                    for tci in range(4):
                        tc = tg * 4 + tci
                        ssl = (grp % 2) * 4 + tci
                        uses = grp // 2
                        if uses >= 1:
                            A.wait_ge(s_stg[ssl].h, 17 * uses)
                        s_aev.inc(A.copy(stage[ssl][:, :], ps_acc[bank + tci][:, 0:NTW]))
                        SC.wait_ge(s_aev.h, s_aev.v)
                        s_stg[ssl].inc(SC.dma_start(
                            out=b_dram.ap()[tc * 128:(tc + 1) * 128,
                                            nt * NTW:(nt + 1) * NTW],
                            in_=stage[ssl][:, :]), 16)
                        V.wait_ge(s_aev.h, s_aev.v)
                        s_stg[ssl].inc(V.bn_stats(
                            bstats[:, (tc * NTN + nt) * 6:(tc * NTN + nt) * 6 + 6],
                            stage[ssl][:, :]))
                    grp += 1
        es_big.close()

        with ExitStack() as es4:
            gw1_sb = es4.enter_context(nc.sbuf_tensor("gw1_sb", [128, KC * 32], BF16))
            A_sb = es4.enter_context(nc.sbuf_tensor("A_sb", [128, KC * 128], BF16))
            t1_sb = es4.enter_context(nc.sbuf_tensor("t1_sb", [128, TOK], BF16))
            t2_sb = es4.enter_context(nc.sbuf_tensor("t2_sb", [128, TOK], BF16))
            bbuf = [es4.enter_context(nc.sbuf_tensor(f"bb{i}", [128, D], BF16))
                    for i in range(2)]
            xbuf4 = [es4.enter_context(nc.sbuf_tensor(f"xb4{i}", [128, D], F32))
                     for i in range(2)]
            c_sb = es4.enter_context(nc.sbuf_tensor("c_sb", [128, D], F32))
            x2_sb = [es4.enter_context(nc.sbuf_tensor(f"x2_{i}", [128, D], F32))
                     for i in range(2)]
            l2bf = [es4.enter_context(nc.sbuf_tensor(f"l2b{i}", [128, D], BF16))
                    for i in range(2)]
            l2t = [es4.enter_context(nc.sbuf_tensor(f"l2t{i}", [128, D], BF16))
                   for i in range(2)]
            ps_tp4 = [es4.enter_context(nc.psum_tensor(f"ps_tp4{i}", [128, 1024], BF16))
                      for i in range(2)]
            ps_tpx = es4.enter_context(nc.psum_tensor("ps_tpx", [128, 1024], BF16))
            ps_h = es4.enter_context(nc.psum_tensor("ps_h", [128, 512], F32))
            ps_tmp = es4.enter_context(nc.psum_tensor("ps_tmp", [128, 512], F32))
            ps_hT = es4.enter_context(nc.psum_tensor("ps_hT", [32, 1024], BF16))
            ps_g = es4.enter_context(nc.psum_tensor("ps_g", [128, 512], F32))
            ps_cnt = es4.enter_context(nc.psum_tensor("ps_cnt", [16, 512], F32))

            G.wait_ge(s_aev.h, s_aev.v)
            s_const.inc(G.dma_start(
                out=gw1_sb[:, :].rearrange("p (kc g) -> p kc g", kc=KC),
                in_=gw1T_d.rearrange("(kc kp) g -> kp kc g", kp=128)), 16)
            s_const.inc(G.dma_start(
                out=A_sb[:, :].rearrange("p (kc er) -> p kc er", kc=KC),
                in_=aRe_d.rearrange("(kc kp) er -> kp kc er", kp=128)), 16)

            SY.wait_ge(s_aev.h, s_aev.v)
            for i in range(8):
                SY.wait_ge(s_stg[i].h, s_stg[i].v)

            v4_prev = 0
            ptc = 0
            for tc in range(TC):
                sl = tc % 2
                if tc >= 2:
                    SY.wait_ge(s_v4.h, s_v4.v)
                    SY.wait_ge(s_p4.h, s_p4.v)
                s_pb[sl].inc(SY.dma_start(
                    out=bbuf[sl][:, :],
                    in_=b_dram.ap()[tc * 128:(tc + 1) * 128, :]), 16)
                s_px[sl].inc(SY.dma_start(
                    out=xbuf4[sl][:, :], in_=x_d[tc * 128:(tc + 1) * 128, :]), 16)

                V.wait_ge(s_pb[sl].h, s_pb[sl].v)
                V.bn_aggr(mv[sl][:, :], bstats[:, tc * 48:(tc + 1) * 48])
                V.drain()
                s_v4.inc(V.tensor_scalar(nm[sl][:, :], mv[sl][:, 0:1], -1.0, None, OP.mult))
                A.wait_ge(s_v4.h, s_v4.v)
                A.activation(rstd[sl][:, :], mv[sl][:, 1:2], AF.Ln,
                             bias=eps_t[:, 0:1], scale=1.0)
                A.drain()
                s_a4.inc(A.activation(rstd[sl][:, :], rstd[sl][:, :], AF.Exp, scale=-0.5))
                V.wait_ge(s_a4.h, s_a4.v)
                V.tensor_tensor(nmr[sl][:, :], nm[sl][:, :], rstd[sl][:, :], OP.mult)
                V.drain()
                V.wait_ge(s_p4.h, s_p4.v)
                s_v4.inc(V.tensor_scalar(c_sb[:, :], bbuf[sl][:, :],
                                         rstd[sl][:, 0:1], nmr[sl][:, 0:1],
                                         OP.mult, OP.add))
                G.wait_ge(s_v4.h, s_v4.v)
                G.wait_ge(s_px[sl].h, s_px[sl].v)
                if tc >= 2:
                    G.wait_ge(s_x2s[sl].h, s_x2s[sl].v)
                s_p4.inc(G.tensor_tensor(x2_sb[sl][:, :], xbuf4[sl][:, :],
                                         c_sb[:, :], OP.add))
                SC.wait_ge(s_p4.h, s_p4.v)
                s_x2s[sl].inc(SC.dma_start(
                    out=x2_d[tc * 128:(tc + 1) * 128, :], in_=x2_sb[sl][:, :]), 16)

                V.wait_ge(s_p4.h, s_p4.v)
                if tc >= 2:
                    V.wait_ge(s_l2u.h, tc - 1)
                ap2 = ln_chain(V, A, x2_sb[sl][:, :], l2bf[sl][:, :], stats[sl],
                               mv[sl], nm[sl], nmr[sl], rstd[sl], eps_t, s_v4, s_a4, 8)
                s_v4.inc(ap2)

                PE.wait_ge(s_v4.h, s_v4.v)
                if tc >= 2:
                    PE.wait_ge(s_l2[sl].h, s_l2[sl].v)
                for kc in range(KC):
                    bank = (ptc // 8) % 2
                    slot = ptc % 8
                    if slot == 0 and ptc >= 16:
                        PE.wait_ge(s_tp4.h, ptc // 8 - 1)
                    s_pe4.inc(PE.transpose(
                        ps_tp4[bank][:, slot * 128:(slot + 1) * 128],
                        l2bf[sl][:, kc * 128:(kc + 1) * 128], ident[:, :]))
                    if kc == KC - 1:
                        s_l2u.inc(PE.nop())
                    if slot == 7:
                        A.wait_ge(s_pe4.h, s_pe4.v)
                        s_tp4.inc(A.copy(
                            l2t[sl][:, (kc - 7) * 128:(kc + 1) * 128],
                            ps_tp4[bank][:, 0:1024]))
                    ptc += 1
                SC.wait_ge(s_tp4.h, s_tp4.v)
                s_l2[sl].inc(SC.dma_start(
                    out=l2t_d[tc * 128:(tc + 1) * 128, :], in_=l2t[sl][:, :]), 16)

                PE.wait_ge(s_tp4.h, s_tp4.v)
                PE.wait_ge(s_v4.h, v4_prev)
                PE.wait_ge(s_const.h, s_const.v)
                mmi = None
                for kc in range(KC):
                    blkap = l2t[sl][:, kc * 128:(kc + 1) * 128]
                    PE.matmul(ps_h[:, 0:32], blkap, gw1_sb[:, kc * 32:(kc + 1) * 32],
                              start=(kc == 0), stop=(kc == KC - 1))
                    mmi = PE.matmul(ps_tmp[:, 0:128], blkap,
                                    A_sb[:, kc * 128:(kc + 1) * 128],
                                    start=(kc == 0), stop=(kc == KC - 1))
                s_pe4.inc(mmi)
                s_l2[sl].inc(PE.nop())
                A.wait_ge(s_pe4.h, s_pe4.v)
                A.copy(h_sb[:, :], ps_h[:, 0:32])
                A.drain()
                s_a4.inc(A.copy(tmp_f[:, :], ps_tmp[:, 0:128]))
                V.wait_ge(s_a4.h, s_a4.v)
                V.tensor_tensor(h_sb[:, :], h_sb[:, :], gb1B[:, :], OP.add)
                V.drain()
                V.bn_stats(hg_st[:, :], h_sb[:, :])
                V.drain()
                V.bn_aggr(hg_mv[:, :], hg_st[:, :])
                V.drain()
                s_v4.inc(V.tensor_scalar(hg_nm[:, :], hg_mv[:, 0:1], -1.0, None, OP.mult))
                A.wait_ge(s_v4.h, s_v4.v)
                A.activation(hg_rs[:, :], hg_mv[:, 1:2], AF.Ln,
                             bias=eps_t[:, 0:1], scale=1.0)
                A.drain()
                s_a4.inc(A.activation(hg_rs[:, :], hg_rs[:, :], AF.Exp, scale=-0.5))
                V.wait_ge(s_a4.h, s_a4.v)
                V.tensor_tensor(hg_nmr[:, :], hg_nm[:, :], hg_rs[:, :], OP.mult)
                V.drain()
                V.tensor_scalar(hn_sb[:, :], h_sb[:, :], hg_rs[:, 0:1],
                                hg_nmr[:, 0:1], OP.mult, OP.add)
                V.drain()
                V.tensor_tensor(hn_sb[:, :], hn_sb[:, :], gamB[:, :], OP.mult)
                V.drain()
                V.tensor_tensor(hn_sb[:, :], hn_sb[:, :], betB[:, :], OP.add)
                V.drain()
                V.tensor_scalar(hn_sb[:, :], hn_sb[:, :], 0.0, None, OP.max)
                V.drain()
                s_v4.inc(V.tensor_copy(hb_sb[:, :], hn_sb[:, :]))
                PE.wait_ge(s_v4.h, s_v4.v)
                s_pe4.inc(PE.transpose(ps_hT[:, 0:128], hb_sb[:, :], ident[:, :]))
                A.wait_ge(s_pe4.h, s_pe4.v)
                s_a4.inc(A.copy(hT_sb[:, :], ps_hT[:, 0:128]))
                PE.wait_ge(s_a4.h, s_a4.v)
                s_pe4.inc(PE.matmul(ps_g[:, 0:8], hT_sb[:, :], gw2_sb[:, :],
                                    start=True, stop=True))
                A.wait_ge(s_pe4.h, s_pe4.v)
                s_a4.inc(A.copy(gates[:, :], ps_g[:, 0:8]))
                V.wait_ge(s_a4.h, s_a4.v)
                V.tensor_tensor(gates[:, :], gates[:, :], gb2B[:, :], OP.add)
                V.drain()
                V.reduce_max(m1[:, :], gates[:, :], axis=AX)
                V.drain()
                V.tensor_scalar(eq1f[:, :], gates[:, :], m1[:, 0:1], None, OP.is_equal)
                V.drain()
                V.tensor_copy(eqb[:, 0:8], eq1f[:, :])
                V.scalar_tensor_tensor(gates2[:, :], eq1f[:, :], BIG_NEG, gates[:, :],
                                       OP.mult, OP.add)
                V.drain()
                V.reduce_max(m2[:, :], gates2[:, :], axis=AX)
                V.drain()
                V.tensor_scalar(eq2f[:, :], gates2[:, :], m2[:, 0:1], None, OP.is_equal)
                V.drain()
                V.tensor_copy(eqb[:, 8:16], eq2f[:, :])
                s_v4.inc(V.scalar_tensor_tensor(dgap[:, :], m1[:, :], -1.0, m2[:, :],
                                                OP.mult, OP.add))
                A.wait_ge(s_v4.h, s_v4.v)
                s_a4.inc(A.activation(sexp[:, :], dgap[:, :], AF.Exp))
                V.wait_ge(s_a4.h, s_a4.v)
                V.tensor_scalar(sp1[:, :], sexp[:, :], 1.0, None, OP.add)
                V.drain()
                V.reciprocal(rs_[:, :], sp1[:, :])
                V.drain()
                V.tensor_tensor(p2[:, :], sexp[:, :], rs_[:, :], OP.mult)
                V.drain()
                V.tensor_scalar(p1[:, :], p2[:, :], -1.0, 1.0, OP.mult, OP.add)
                V.drain()
                V.tensor_scalar(g1[:, :], eq1f[:, :], p1[:, 0:1], ALPHA, OP.mult, OP.mult)
                V.drain()
                V.tensor_scalar(g2[:, :], eq2f[:, :], p2[:, 0:1], ALPHA, OP.mult, OP.mult)
                V.drain()
                g1v = g1[:, :].unsqueeze(2).broadcast_to([128, 8, 16])
                g2v = g2[:, :].unsqueeze(2).broadcast_to([128, 8, 16])
                tview = tmp_f[:, :].rearrange("p (ee r) -> p ee r", ee=8)
                V.tensor_tensor(tmp1b[:, :].rearrange("p (ee r) -> p ee r", ee=8),
                                tview, g1v, OP.mult)
                V.drain()
                s_v4.inc(V.tensor_tensor(
                    tmp2b[:, :].rearrange("p (ee r) -> p ee r", ee=8),
                    tview, g2v, OP.mult))
                PE.wait_ge(s_v4.h, s_v4.v)
                for qi, (tsrc, tdst) in enumerate(((tmp1b, t1_sb), (tmp2b, t2_sb))):
                    s_pe4.inc(PE.transpose(ps_tpx[:, qi * 128:(qi + 1) * 128],
                                           tsrc[:, :], ident[:, :]))
                    A.wait_ge(s_pe4.h, s_pe4.v)
                    s_a4.inc(A.copy(tdst[:, tc * 128:(tc + 1) * 128],
                                    ps_tpx[:, qi * 128:(qi + 1) * 128]))
                s_pe4.inc(PE.matmul(ps_cnt[:, 0:1], eqb[:, :], ones_b[:, :],
                                    start=True, stop=True))
                A.wait_ge(s_pe4.h, s_pe4.v)
                s_a4.inc(A.copy(cnt_tmp[:, :], ps_cnt[:, 0:1]))
                V.wait_ge(s_a4.h, s_a4.v)
                s_v4.inc(V.tensor_tensor(cnt_acc[:, :], cnt_acc[:, :],
                                         cnt_tmp[:, :], OP.add))
                V.drain()
                v4_prev = s_v4.v

            SC.wait_ge(s_v4.h, s_v4.v)
            SC.wait_ge(s_a4.h, s_a4.v)
            s_out.inc(SC.dma_start(out=cnt_d, in_=cnt_acc[:, :]), 16)
            s_out.inc(SC.dma_start(out=t1_d, in_=t1_sb[:, :]), 16)
            s_out.inc(SC.dma_start(out=t2_d, in_=t2_sb[:, :]), 16)
            for sem in s_x2s + s_l2:
                SC.wait_ge(sem.h, sem.v)
            SC.wait_ge(s_out.h, s_out.v)
        es_bst.close()

    return nc



def build_B():
    nc = bass.Bass("TRN2", target_bir_lowering=False, debug=False)

    cnts_d = nc.dram_tensor("cnts", [16, 8], F32, kind="ExternalInput").ap()
    x2_d = nc.dram_tensor("x2i", [TOK, D], F32, kind="ExternalInput").ap()
    l2t_d = nc.dram_tensor("l2ti", [TOK, D], BF16, kind="ExternalInput").ap()
    t1_d = nc.dram_tensor("t1i", [128, TOK], BF16, kind="ExternalInput").ap()
    t2_d = nc.dram_tensor("t2i", [128, TOK], BF16, kind="ExternalInput").ap()
    wfT_d = nc.dram_tensor("wfT", [D, D], F32, kind="ExternalInput").ap()
    bRe_d = nc.dram_tensor("bRe", [128, D], F32, kind="ExternalInput").ap()

    out_d = nc.dram_tensor("out", [TOK, D], F32, kind="ExternalOutput").ap()
    e_dram = nc.dram_tensor("e_scr", [TOK, D], BF16)

    PE, V, A, G, SY, SC = nc.tensor, nc.vector, nc.scalar, nc.gpsimd, nc.sync, nc.scalar

    with ExitStack() as es:
        e = es.enter_context

        eps_t = e(nc.sbuf_tensor("eps_t", [128, 1], F32))
        ea0 = e(nc.sbuf_tensor("ea0", [16, 128], BF16))
        ea1 = e(nc.sbuf_tensor("ea1", [16, 128], BF16))
        cnts = e(nc.sbuf_tensor("cnts_s", [16, 8], F32))
        tot = e(nc.sbuf_tensor("tot", [16, 1], F32))
        alw = e(nc.sbuf_tensor("alw", [16, 1], F32))
        alwb = e(nc.sbuf_tensor("alwb", [16, 1], BF16))
        a0x = e(nc.sbuf_tensor("a0x", [128, 1], F32))
        a1x = e(nc.sbuf_tensor("a1x", [128, 1], F32))
        estats = e(nc.sbuf_tensor("estats", [128, TC * NTN * 6], F32))
        stats = [e(nc.sbuf_tensor(f"st{i}", [128, 96], F32)) for i in range(2)]
        mv = [e(nc.sbuf_tensor(f"mv{i}", [128, 2], F32)) for i in range(2)]
        nm = [e(nc.sbuf_tensor(f"nm{i}", [128, 1], F32)) for i in range(2)]
        nmr = [e(nc.sbuf_tensor(f"nmr{i}", [128, 1], F32)) for i in range(2)]
        rstd = [e(nc.sbuf_tensor(f"rs{i}", [128, 1], F32)) for i in range(2)]

        s_ld = Sem(es, nc, "s_ld")
        s_q1 = Sem(es, nc, "s_q1")
        s_q1a = Sem(es, nc, "s_q1a")
        s_q1p = Sem(es, nc, "s_q1p")
        s_l2 = Sem(es, nc, "s_l2")
        s_wf = [Sem(es, nc, f"s_wf{i}") for i in range(NWS)]
        s_wff = [Sem(es, nc, f"s_wff{i}") for i in range(NWS)]
        s_mm = Sem(es, nc, "s_mm")
        s_aev = Sem(es, nc, "s_aev")
        s_stg = [Sem(es, nc, f"s_stg{i}") for i in range(8)]
        s_pe = [Sem(es, nc, f"s_pe{i}") for i in range(2)]
        s_pxx = [Sem(es, nc, f"s_pxx{i}") for i in range(2)]
        s_v3 = Sem(es, nc, "s_v3")
        s_a3 = Sem(es, nc, "s_a3")
        s_p3 = Sem(es, nc, "s_p3")
        s_out = Sem(es, nc, "s_out")

        blk = e(nc.Block())

        es_t = ExitStack()
        t1 = es_t.enter_context(nc.sbuf_tensor("t1", [128, TOK], BF16, side="right"))
        t2 = es_t.enter_context(nc.sbuf_tensor("t2", [128, TOK], BF16, side="right"))

        es_q = ExitStack()
        tall = es_q.enter_context(nc.sbuf_tensor("tall", [128, TOK], BF16))
        b_sb = es_q.enter_context(nc.sbuf_tensor("b_sb", [128, D], BF16))
        l2T = es_q.enter_context(nc.sbuf_tensor("l2T", [128, KC * TOK], BF16))

        V.memset(eps_t[:, :], EPS)
        G.memset(ea0[:, :], 1.0)
        G.affine_select(out=ea0[:, :], in_=ea0[:, :], compare_op=OP.is_ge,
                        fill=0.0, base=0, pattern=[[1, 128]], channel_multiplier=-16)
        G.affine_select(out=ea0[:, :], in_=ea0[:, :], compare_op=OP.is_ge,
                        fill=0.0, base=15, pattern=[[-1, 128]], channel_multiplier=16)
        G.memset(ea1[:, :], 1.0)
        G.affine_select(out=ea1[:, :], in_=ea1[:, :], compare_op=OP.is_ge,
                        fill=0.0, base=128, pattern=[[1, 128]], channel_multiplier=-16)
        s_q1p.inc(G.affine_select(out=ea1[:, :], in_=ea1[:, :], compare_op=OP.is_ge,
                                  fill=0.0, base=-113, pattern=[[-1, 128]],
                                  channel_multiplier=16))
        s_ld.inc(SY.dma_start(out=cnts[:, :], in_=cnts_d), 16)
        s_ld.inc(SY.dma_start(out=t1[:, :], in_=t1_d), 16)
        s_ld.inc(SY.dma_start(out=t2[:, :], in_=t2_d), 16)
        s_q1.inc(G.dma_start(out=b_sb[:, :], in_=bRe_d), 16)

        V.wait_ge(s_ld.h, s_ld.v)
        V.tensor_reduce(tot[:, :], cnts[:, :], axis=AX, op=OP.add)
        V.drain()
        V.tensor_scalar(alw[:, :], tot[:, :], CAPACITY + 0.5, None, OP.is_lt)
        V.drain()
        s_v3.inc(V.tensor_copy(alwb[:, :], alw[:, :]))
        with ExitStack() as esq:
            ps_e = esq.enter_context(nc.psum_tensor("ps_e", [128, 512], F32))
            PE.wait_ge(s_v3.h, s_v3.v)
            PE.wait_ge(s_q1p.h, s_q1p.v)
            s_q1.inc(PE.matmul(ps_e[:, 0:1], ea0[:, :], alwb[:, :],
                               start=True, stop=True), 1)
            A.wait_ge(s_q1.h, s_q1.v)
            s_q1a.inc(A.copy(a0x[:, :], ps_e[:, 0:1]))
            PE.wait_ge(s_q1a.h, s_q1a.v)
            s_q1.inc(PE.matmul(ps_e[:, 0:1], ea1[:, :], alwb[:, :],
                               start=True, stop=True), 1)
            A.wait_ge(s_q1.h, s_q1.v)
            s_q1a.inc(A.copy(a1x[:, :], ps_e[:, 0:1]))
        V.wait_ge(s_q1a.h, s_q1a.v)
        V.tensor_scalar(tall[:, :], t2[:, :], a1x[:, 0:1], None, OP.mult)
        V.drain()
        s_v3.inc(V.scalar_tensor_tensor(tall[:, :], t1[:, :], a0x[:, 0:1],
                                        tall[:, :], OP.mult, OP.add))
        es_t.close()

        l2Tv = l2T[:, :].rearrange("p (kc t) -> p kc t", kc=KC)
        for tc in range(TC):
            s_l2.inc(SY.dma_start(
                out=l2Tv[:, :, tc * 128:(tc + 1) * 128],
                in_=l2t_d[tc * 128:(tc + 1) * 128, :]
                    .rearrange("p (kc t) -> p kc t", kc=KC)), 16)

        with ExitStack() as es3:
            wf = [es3.enter_context(nc.sbuf_tensor(f"wf{i}", [128, KH * NTW], BF16))
                  for i in range(NWS)]
            stage = [es3.enter_context(nc.sbuf_tensor(f"stg{i}", [128, NTW], BF16))
                     for i in range(8)]
            ps_acc = [es3.enter_context(nc.psum_tensor(f"ps_acc{i}", [128, 512], F32))
                      for i in range(8)]

            grp = 0
            for nt in range(NTN):
                for kh in range(2):
                    sidx = nt * 2 + kh
                    wsl = sidx % NWS
                    if sidx >= NWS:
                        G.wait_ge(s_wff[wsl].h, s_wff[wsl].v)
                    s_wf[wsl].inc(G.dma_start(
                        out=wf[wsl][:, :].rearrange("p (kc n) -> p kc n", kc=KH),
                        in_=wfT_d[kh * KH * 128:(kh + 1) * KH * 128,
                                  nt * NTW:(nt + 1) * NTW]
                            .rearrange("(kc kp) n -> kp kc n", kp=128)), 16)
                if nt == 0:
                    PE.wait_ge(s_l2.h, s_l2.v)
                    PE.wait_ge(s_v3.h, s_v3.v)
                    PE.wait_ge(s_q1.h, s_q1.v)
                    PE.wait_ge(s_q1a.h, s_q1a.v)
                for tg in range(TG):
                    bank = (grp % 2) * 4
                    if grp >= 2:
                        PE.wait_ge(s_aev.h, 4 * (grp - 1))
                    for kc in range(KC):
                        wsl = (nt * 2 + kc // KH) % NWS
                        if kc % KH == 0:
                            PE.wait_ge(s_wf[wsl].h, s_wf[wsl].v)
                        for tci in range(4):
                            tc = tg * 4 + tci
                            PE.matmul(
                                ps_acc[bank + tci][:, 0:NTW],
                                l2T[:, kc * TOK + tc * 128:kc * TOK + (tc + 1) * 128],
                                wf[wsl][:, (kc % KH) * NTW:(kc % KH + 1) * NTW],
                                start=(kc == 0), stop=False)
                    mmi = None
                    for tci in range(4):
                        tc = tg * 4 + tci
                        mmi = PE.matmul(ps_acc[bank + tci][:, 0:NTW],
                                        tall[:, tc * 128:(tc + 1) * 128],
                                        b_sb[:, nt * NTW:(nt + 1) * NTW],
                                        start=False, stop=True)
                    s_mm.inc(mmi)
                    if tg == TG - 1:
                        s_wff[(nt * 2) % NWS].inc(PE.nop())
                        s_wff[(nt * 2 + 1) % NWS].inc(PE.nop())
                    A.wait_ge(s_mm.h, s_mm.v)
                    for tci in range(4):
                        tc = tg * 4 + tci
                        ssl = (grp % 2) * 4 + tci
                        uses = grp // 2
                        if uses >= 1:
                            A.wait_ge(s_stg[ssl].h, 17 * uses)
                        s_aev.inc(A.copy(stage[ssl][:, :], ps_acc[bank + tci][:, 0:NTW]))
                        SC.wait_ge(s_aev.h, s_aev.v)
                        s_stg[ssl].inc(SC.dma_start(
                            out=e_dram.ap()[tc * 128:(tc + 1) * 128,
                                            nt * NTW:(nt + 1) * NTW],
                            in_=stage[ssl][:, :]), 16)
                        V.wait_ge(s_aev.h, s_aev.v)
                        s_stg[ssl].inc(V.bn_stats(
                            estats[:, (tc * NTN + nt) * 6:(tc * NTN + nt) * 6 + 6],
                            stage[ssl][:, :]))
                    grp += 1
        es_q.close()

        with ExitStack() as es4:
            ebuf = [es4.enter_context(nc.sbuf_tensor(f"eb{i}", [128, D], BF16))
                    for i in range(2)]
            x2b = [es4.enter_context(nc.sbuf_tensor(f"x2b{i}", [128, D], F32))
                   for i in range(2)]
            f_sb = es4.enter_context(nc.sbuf_tensor("f_sb", [128, D], F32))
            o_sb = [es4.enter_context(nc.sbuf_tensor(f"o_sb{i}", [128, D], F32))
                    for i in range(2)]

            SY.wait_ge(s_aev.h, s_aev.v)
            for i in range(8):
                SY.wait_ge(s_stg[i].h, s_stg[i].v)
            for tc in range(TC):
                sl = tc % 2
                if tc >= 2:
                    SY.wait_ge(s_p3.h, s_p3.v)
                    SY.wait_ge(s_v3.h, s_v3.v)
                s_pe[sl].inc(SY.dma_start(
                    out=ebuf[sl][:, :],
                    in_=e_dram.ap()[tc * 128:(tc + 1) * 128, :]), 16)
                s_pxx[sl].inc(SY.dma_start(
                    out=x2b[sl][:, :], in_=x2_d[tc * 128:(tc + 1) * 128, :]), 16)
                V.wait_ge(s_pe[sl].h, s_pe[sl].v)
                V.bn_aggr(mv[sl][:, :], estats[:, tc * 48:(tc + 1) * 48])
                V.drain()
                s_v3.inc(V.tensor_scalar(nm[sl][:, :], mv[sl][:, 0:1], -1.0, None, OP.mult))
                A.wait_ge(s_v3.h, s_v3.v)
                A.activation(rstd[sl][:, :], mv[sl][:, 1:2], AF.Ln,
                             bias=eps_t[:, 0:1], scale=1.0)
                A.drain()
                s_a3.inc(A.activation(rstd[sl][:, :], rstd[sl][:, :], AF.Exp, scale=-0.5))
                V.wait_ge(s_a3.h, s_a3.v)
                V.tensor_tensor(nmr[sl][:, :], nm[sl][:, :], rstd[sl][:, :], OP.mult)
                V.drain()
                V.wait_ge(s_p3.h, s_p3.v)
                s_v3.inc(V.tensor_scalar(f_sb[:, :], ebuf[sl][:, :],
                                         rstd[sl][:, 0:1], nmr[sl][:, 0:1],
                                         OP.mult, OP.add))
                G.wait_ge(s_v3.h, s_v3.v)
                G.wait_ge(s_pxx[sl].h, s_pxx[sl].v)
                if tc >= 2:
                    G.wait_ge(s_out.h, s_out.v - 16)
                s_p3.inc(G.tensor_tensor(o_sb[sl][:, :], x2b[sl][:, :],
                                         f_sb[:, :], OP.add))
                SC.wait_ge(s_p3.h, s_p3.v)
                s_out.inc(SC.dma_start(
                    out=out_d[tc * 128:(tc + 1) * 128, :], in_=o_sb[sl][:, :]), 16)
            SC.wait_ge(s_out.h, s_out.v)

    return nc



def kernel(L0_attn_data, L0_attn_weight, L0_attn_mask, L0_ffn_weight,
           L0_ffn_lora_A, L0_ffn_lora_B, L0_ffn_gw1, L0_ffn_gb1,
           L0_ffn_ln_gamma, L0_ffn_ln_beta, L0_ffn_gw2, L0_ffn_gb2):
    x = np.asarray(L0_attn_data, dtype=np.float32)
    waT = np.ascontiguousarray(np.asarray(L0_attn_weight, np.float32).T)
    wfT = np.ascontiguousarray(np.asarray(L0_ffn_weight, np.float32).T)
    mask = np.asarray(L0_attn_mask, np.int32)
    maskR = np.ascontiguousarray(
        mask.reshape(128, 32, 32, 4, 32).transpose(3, 2, 0, 4, 1).reshape(128, 128, 1024))
    aRe = np.ascontiguousarray(
        np.asarray(L0_ffn_lora_A, np.float32).transpose(1, 0, 2).reshape(D, 128))
    bRe = np.ascontiguousarray(np.asarray(L0_ffn_lora_B, np.float32).reshape(128, D))
    gw1T = np.ascontiguousarray(np.asarray(L0_ffn_gw1, np.float32).T)
    gw2T = np.ascontiguousarray(np.asarray(L0_ffn_gw2, np.float32).T)
    gb1B = np.ascontiguousarray(np.broadcast_to(
        np.asarray(L0_ffn_gb1, np.float32), (128, 32)))
    gamB = np.ascontiguousarray(np.broadcast_to(
        np.asarray(L0_ffn_ln_gamma, np.float32), (128, 32)))
    betB = np.ascontiguousarray(np.broadcast_to(
        np.asarray(L0_ffn_ln_beta, np.float32), (128, 32)))
    gb2B = np.ascontiguousarray(np.broadcast_to(
        np.asarray(L0_ffn_gb2, np.float32), (128, 8)))

    ncA = build_A()
    in_maps_A = [{
        "x": np.ascontiguousarray(x[c]),
        "waT": waT, "maskR": maskR, "aRe": aRe,
        "gw1T": gw1T, "gw2T": gw2T,
        "gb1B": gb1B, "gamB": gamB, "betB": betB, "gb2B": gb2B,
    } for c in range(NCORES)]
    resA = run_bass_kernel_spmd(ncA, in_maps_A, core_ids=list(range(NCORES)))

    cnts = np.ascontiguousarray(
        np.concatenate([resA.results[c]["counts"] for c in range(NCORES)], axis=1))

    ncB = build_B()
    in_maps_B = [{
        "cnts": cnts,
        "x2i": resA.results[c]["x2o"],
        "l2ti": resA.results[c]["l2to"],
        "t1i": resA.results[c]["t1o"],
        "t2i": resA.results[c]["t2o"],
        "wfT": wfT, "bRe": bRe,
    } for c in range(NCORES)]
    resB = run_bass_kernel_spmd(ncB, in_maps_B, core_ids=list(range(NCORES)))

    out = np.stack([resB.results[c]["out"] for c in range(NCORES)], axis=0)
    return np.ascontiguousarray(out, dtype=np.float32)
